# version 05
# speedup vs baseline: 1.2755x; 1.0651x over previous
import contextlib
import numpy as np
import ml_dtypes
import concourse.bass as bass
import concourse.mybir as mybir
from concourse.bass_utils import run_bass_kernel_spmd

F32 = mybir.dt.float32
BF16 = mybir.dt.bfloat16
AF = mybir.ActivationFunctionType
ALU = mybir.AluOpType
AX = mybir.AxisListType
bf16 = ml_dtypes.bfloat16

NCORE = 8
SPC = 512
B = 4
S = 4096
DM = 1024
TPC = B * SPC

COMPUTE_ENGS = ("T", "V", "S", "P")
DMA_ROT = 8


class Prog:

    def __init__(self, name):
        self.nc = bass.Bass("TRN2", target_bir_lowering=False)
        self.name = name
        self.streams = {e: [] for e in COMPUTE_ENGS + ("Q",)}
        self.semcnt = {e: 0 for e in COMPUTE_ENGS + ("Q",)}
        self.dmarot = {}
        for q in ("Q", "P", "S"):
            for i in range(DMA_ROT):
                self.semcnt[f"{q}d{i}"] = 0
            self.dmarot[q] = 0
        self._n = 0
        self.bank_free = {}

    def sb(self, shape, dtype, name=None):
        self._n += 1
        nm = f"{name}_s{self._n}" if name else f"sb{self._n}"
        return self.nc.alloc_sbuf_tensor(nm, shape, dtype)

    def psum(self, shape, dtype=F32, name=None):
        self._n += 1
        nm = f"{name}_p{self._n}" if name else f"ps{self._n}"
        return self.nc.alloc_psum_tensor(nm, shape, dtype)

    def din(self, name, shape, dtype=F32):
        return self.nc.dram_tensor(name, shape, dtype, kind="ExternalInput").ap()

    def dout(self, name, shape, dtype=F32):
        return self.nc.dram_tensor(name, shape, dtype, kind="ExternalOutput").ap()

    def dtmp(self, name, shape, dtype):
        return self.nc.dram_tensor(name, shape, dtype, kind="Internal").ap()

    def pw(self, *banks):
        return tuple(self.bank_free.get(b) for b in banks)

    def pset(self, banks, tok):
        for b in banks:
            self.bank_free[b] = tok

    def on(self, eng, fn, after=(), inc=0):
        after = tuple(t for t in after if t is not None)
        if inc:
            self.semcnt[eng] += inc
        self.streams[eng].append((after, fn, (eng, inc) if inc else None))
        return (eng, self.semcnt[eng]) if inc else None

    def dma(self, q, out, in_, after=(), **kw):
        after = tuple(t for t in after if t is not None)
        i = self.dmarot[q]
        self.dmarot[q] += 1
        sname = f"{q}d{i % DMA_ROT}"
        nprev = i // DMA_ROT
        pre = ((sname, 16 * nprev),) if nprev else ()
        self.semcnt[sname] += 16
        tok = (sname, self.semcnt[sname])

        def fn(E):
            return E.dma_start(out=out, in_=in_, **kw)
        self.streams[q].append((pre + after, fn, (sname, 16)))
        return tok

    def barrier_token(self, eng, after):
        return self.on(eng, lambda E: E.nop(), after=after, inc=1)

    def finalize(self, final_tokens):
        self.on("Q", lambda E: E.nop(), after=final_tokens, inc=0)
        nc = self.nc
        es = contextlib.ExitStack()
        sems = {name: es.enter_context(nc.semaphore(f"sm_{name}"))
                for name in self.semcnt}
        block = es.enter_context(nc.Block())

        def body(eng_name):
            def run(E):
                waited = {}
                for after, fn, inc in self.streams[eng_name]:
                    for (s2, v) in after:
                        if v > waited.get(s2, 0):
                            E.wait_ge(sems[s2], v)
                            waited[s2] = v
                    inst = fn(E)
                    if inc is not None:
                        s, n = inc
                        if inst is None:
                            inst = E.nop()
                        inst.then_inc(sems[s], n)
            return run

        block.tensor(body("T"))
        block.vector(body("V"))
        block.scalar(body("S"))
        block.gpsimd(body("P"))
        block.sync(body("Q"))
        es.close()
        return nc



def emit_ln_tile(p, x_ap, y_ap, sc, t, after=(), prev_cent_reader=None,
                 gamma_col=None, beta_col=None, eps=1e-5):
    D = x_ap.shape[-1]
    c0 = sc["col4"]
    negmean = c0[:, 4 * t:4 * t + 1]
    varep = c0[:, 4 * t + 1:4 * t + 2]
    lnv = c0[:, 4 * t + 2:4 * t + 3]
    rstd = c0[:, 4 * t + 3:4 * t + 4]
    cent, sq = sc["cent"][:, 0:D], sc["sq"][:, 0:D]
    t1 = p.on("V", lambda E: E.tensor_reduce(
        negmean, x_ap, AX.X, ALU.add, negate=True), after=after, inc=1)
    t2 = p.on("V", lambda E: E.tensor_scalar_mul(negmean, negmean, 1.0 / D),
              after=(t1,), inc=1)
    t3 = p.on("S", lambda E: E.activation(cent, x_ap, AF.Identity, bias=negmean),
              after=(t2, prev_cent_reader) + tuple(after), inc=1)
    t4 = p.on("S", lambda E: E.activation(sq, cent, AF.Square, accum_out=varep),
              after=(t3,), inc=1)
    t5 = p.on("V", lambda E: E.tensor_scalar(varep, varep, 1.0 / D, eps,
                                             ALU.mult, ALU.add), after=(t4,), inc=1)
    t6 = p.on("S", lambda E: E.activation(lnv, varep, AF.Ln), after=(t5,), inc=1)
    t7 = p.on("S", lambda E: E.activation(rstd, lnv, AF.Exp, scale=-0.5),
              after=(t6,), inc=1)
    if gamma_col is not None:
        rg = c0[:, 4 * t + 2:4 * t + 3]
        t8 = p.on("V", lambda E: E.tensor_scalar(rg, gamma_col, rstd, None,
                                                 ALU.mult), after=(t7,), inc=1)
        bexp = beta_col.to_broadcast([128, D])
        ty = p.on("V", lambda E: E.scalar_tensor_tensor(
            y_ap, cent, rg, bexp, ALU.mult, ALU.add), after=(t8,), inc=1)
    else:
        ty = p.on("V", lambda E: E.tensor_scalar_mul(y_ap, cent, rstd),
                  after=(t7,), inc=1)
    return ty, t3



def build_l0_ffn(n_experts=8, halves=2, do_tp=True, do_router=True, rt_level=5):
    p = Prog("l0ffn")
    E_ = 8
    x_h = p.din("x", [TPC, DM])
    wr_h = p.din("wr", [128, 8, E_])
    w1_h = p.din("w1", [E_, 32, 128, 8, 128])
    b1_h = p.din("b1", [128, E_, 32])
    w2_h = p.din("w2", [E_, 32, 2, 128, 512])
    b2_h = p.din("b2", [1, E_, DM])
    id_h = p.din("ident", [128, 128], BF16)
    on_h = p.din("onesr", [1, 128], BF16)
    y_h = p.dout("y", [TPC, DM])

    NT = 8
    TH = NT * 128

    xg = p.sb([128, NT, DM], F32, "xg")
    ybf = p.sb([128, NT, DM], BF16, "ybf")
    yfm = p.sb([128, 8, TH], BF16, "yfm")
    hfm = p.sb([128, 32, TH], BF16, "hfm")
    w1c = [p.sb([128, 8, 128], BF16, f"w1c{i}") for i in range(2)]
    w2c = [p.sb([128, 512], BF16, f"w2c{i}") for i in range(2)]
    wr = p.sb([128, 8, E_], BF16, "wr")
    b1 = p.sb([128, E_, 32], F32, "b1")
    b2 = p.sb([1, E_, DM], BF16, "b2")
    ident = p.sb([128, 128], BF16, "ident")
    onesr = p.sb([1, 128], BF16, "onesr")
    mask = p.sb([128, NT, E_], F32, "mask")
    cent = p.sb([128, DM], F32, "cent")
    sq = p.sb([128, DM], F32, "sq")
    col4 = p.sb([128, 4 * NT], F32, "col4")
    rcol = p.sb([128, 4], F32, "rcol")

    pall = p.psum([128, 4096], F32, "pall")
    tp_ps = pall[:, 0:256].bitcast(BF16)

    sc = {"cent": cent, "sq": sq, "col4": col4}

    tk_id = p.dma("Q", ident[:, :], id_h[:, :])
    tk_on = p.dma("Q", onesr[:, :], on_h[:, :])
    tk_wr = p.dma("P", wr[:, :, :], wr_h[:, :, :])
    tk_b1 = p.dma("Q", b1[:, :, :], b1_h[:, :, :])
    tk_b2 = p.dma("P", b2[:, :, :], b2_h[:, :, :])

    store_toks = []
    xg_free = None
    ybf_free = None
    yfm_free = None
    hfm_free = None
    w1c_free = [None, None]
    w2c_free = [None, None]

    for half in range(halves):
        t0 = half * TH
        src = bass.AP(x_h.tensor, t0 * DM, [[DM, 128], [128 * DM, NT], [1, DM]])
        tk_x = p.dma("Q", xg[:, :, :], src, after=(xg_free,))

        ln_toks = []
        prev_cr = None
        for t in range(NT):
            ty, _ = emit_ln_tile(p, xg[:, t, :], ybf[:, t, :], sc, t,
                                 after=(tk_x, ybf_free),
                                 prev_cent_reader=prev_cr)
            prev_cr = ty
            ln_toks.append(ty)

        if not do_tp:
            lnb = p.barrier_token("V", (ln_toks[-1],))
            dst = bass.AP(y_h.tensor, t0 * DM,
                          [[DM, 128], [128 * DM, NT], [1, DM]])
            st = p.dma("Q", dst, xg[:, :, :], after=(lnb,))
            store_toks.append(st)
            xg_free = st
            continue
        tp_toks = []
        for j in range(8):
            for t in range(NT):
                blk = t % 4
                tt = p.on("T", lambda E, j=j, t=t, blk=blk: E.transpose(
                    tp_ps[:, 128 * blk:128 * blk + 128],
                    ybf[:, t, 128 * j:128 * j + 128], ident[:, :]),
                    after=(ln_toks[t], tk_id) + p.pw(0),
                    inc=(1 if blk == 3 else 0))
                if blk == 3:
                    te = p.on("S", lambda E, j=j, t=t: E.activation(
                        yfm[:, j, 128 * (t - 3):128 * (t + 1)], tp_ps,
                        AF.Identity), after=(tt, yfm_free), inc=1)
                    p.pset([0], te)
                    tp_toks.append(te)
        ybf_free = tp_toks[-1]
        tp_done = p.barrier_token("S", (tp_toks[-1],))

        if not do_router:
            dst = bass.AP(y_h.tensor, t0 * DM,
                          [[DM, 128], [128 * DM, NT], [1, DM]])
            st = p.dma("Q", dst, xg[:, :, :], after=(tp_done,))
            store_toks.append(st)
            xg_free = st
            continue
        rt_last = None
        rt_banks = []
        for t in range(NT):
            lgs = pall[:, 512 * t:512 * t + E_]
            mm = None
            for j in range(8):
                mm = p.on("T", lambda E, j=j, t=t, lgs=lgs: E.matmul(
                    lgs, yfm[:, j, 128 * t:128 * t + 128], wr[:, j, :],
                    start=(j == 0), stop=(j == 7)),
                    after=(tp_done, tk_wr) + p.pw(t),
                    inc=(1 if j == 7 else 0))
            neglm = rcol[:, 0:1]
            sume = rcol[:, 1:2]
            wgt = rcol[:, 2:3]
            tprev = mm
            if rt_level >= 2:
                tprev = p.on("V", lambda E, lgs=lgs, neglm=neglm: E.reduce_max(
                    neglm, lgs, AX.X, negate=True), after=(mm,), inc=1)
            t1 = tprev
            if rt_level >= 3:
                tprev = p.on("S", lambda E, lgs=lgs, neglm=neglm, sume=sume:
                             E.activation(sq[:, 0:E_], lgs, AF.Exp, bias=neglm,
                                          accum_out=sume), after=(tprev,), inc=1)
                tprev = p.on("V", lambda E, sume=sume, wgt=wgt: E.reciprocal(
                    wgt, sume), after=(tprev,), inc=1)
            if rt_level >= 4:
                tprev = p.on("V", lambda E, lgs=lgs, neglm=neglm, t=t: E.tensor_scalar(
                    mask[:, t, :], lgs, neglm, 0.0, ALU.add, ALU.is_equal),
                    after=(tprev, t1), inc=1)
            if rt_level >= 5:
                tprev = p.on("V", lambda E, t=t, wgt=wgt: E.tensor_scalar_mul(
                    mask[:, t, :], mask[:, t, :], wgt), after=(tprev,), inc=1)
            if rt_level < 4:
                tprev = p.on("V", lambda E, lgs=lgs, t=t: E.tensor_copy(
                    mask[:, t, :], lgs), after=(tprev, t1), inc=1)
            p.pset([t], tprev)
            rt_banks.append(tprev)
            rt_last = tprev
        rt_done = p.barrier_token("V", tuple(rt_banks))

        for e in range(n_experts):
            h_last = None
            for c in range(32):
                buf = c % 2
                tk_w1 = p.dma("P", w1c[buf][:, :, :], w1_h[e, c, :, :, :],
                              after=(w1c_free[buf],))
                for th in range(2):
                    bank = 2 + th
                    hp = pall[:, 512 * bank:512 * (bank + 1)]
                    mm = None
                    for j in range(8):
                        mm = p.on("T", lambda E, j=j, th=th, buf=buf, hp=hp: E.matmul(
                            hp, w1c[buf][:, j, :],
                            yfm[:, j, 512 * th:512 * (th + 1)],
                            start=(j == 0), stop=(j == 7)),
                            after=(tk_w1, tp_done) + p.pw(bank),
                            inc=(1 if j == 7 else 0))
                    te = p.on("S", lambda E, c=c, th=th, e=e, hp=hp: E.activation(
                        hfm[:, c, 512 * th:512 * (th + 1)], hp, AF.Relu,
                        bias=b1[:, e, c:c + 1]),
                        after=(mm, tk_b1, hfm_free), inc=1)
                    p.pset([bank], te)
                    h_last = te
                w1c_free[buf] = h_last
            h_done = h_last

            for dh in range(2):
                for u in range(NT):
                    yp = pall[:, 512 * u:512 * (u + 1)]
                    p.on("T", lambda E, u=u, e=e, dh=dh, yp=yp: E.matmul(
                        yp, onesr[:, :], b2[:, e, 512 * dh:512 * (dh + 1)],
                        start=True, stop=False),
                        after=(tk_on, tk_b2, h_done, rt_done) + p.pw(u), inc=0)
                mm_last = [None] * NT
                for c in range(32):
                    buf = c % 2
                    tk_w2 = p.dma("P", w2c[buf][:, :], w2_h[e, c, dh, :, :],
                                  after=(w2c_free[buf],))
                    for u in range(NT):
                        yp = pall[:, 512 * u:512 * (u + 1)]
                        mm_last[u] = p.on(
                            "T", lambda E, c=c, u=u, buf=buf, yp=yp: E.matmul(
                                yp, hfm[:, c, 128 * u:128 * (u + 1)],
                                w2c[buf][:, :], start=False, stop=(c == 31)),
                            after=(tk_w2, h_done),
                            inc=(1 if (c == 31 or u == NT - 1) else 0))
                    w2c_free[buf] = mm_last[NT - 1]
                for u in range(NT):
                    yp = pall[:, 512 * u:512 * (u + 1)]
                    tc = p.on("V", lambda E, u=u, e=e, dh=dh, yp=yp:
                              E.scalar_tensor_tensor(
                                  xg[:, u, 512 * dh:512 * (dh + 1)], yp,
                                  mask[:, u, e:e + 1],
                                  xg[:, u, 512 * dh:512 * (dh + 1)],
                                  ALU.mult, ALU.add),
                              after=(mm_last[u], rt_done), inc=1)
                    p.pset([u], tc)
                hfm_free = mm_last[NT - 1]
            yfm_free = h_done

        comb_done = p.barrier_token("V", p.pw(*range(8)) + (rt_done,))
        dst = bass.AP(y_h.tensor, t0 * DM, [[DM, 128], [128 * DM, NT], [1, DM]])
        st = p.dma("Q", dst, xg[:, :, :], after=(comb_done,))
        store_toks.append(st)
        xg_free = st

    return p.finalize(store_toks)


def _l0ffn_inputs(x_own, w_router, w1, b1, w2, b2):
    E_ = 8
    return {
        "x": np.ascontiguousarray(x_own, dtype=np.float32),
        "wr": np.ascontiguousarray(
            np.asarray(w_router).reshape(8, 128, E_).transpose(1, 0, 2)),
        "w1": np.ascontiguousarray(
            np.asarray(w1).reshape(E_, 8, 128, 32, 128).transpose(0, 3, 2, 1, 4)),
        "b1": np.ascontiguousarray(
            np.asarray(b1).reshape(E_, 32, 128).transpose(2, 0, 1)),
        "w2": np.ascontiguousarray(
            np.asarray(w2).reshape(E_, 32, 128, 2, 512).transpose(0, 1, 3, 2, 4)),
        "b2": np.ascontiguousarray(np.asarray(b2).reshape(1, E_, DM)),
        "ident": np.eye(128, dtype=np.float32).astype(bf16),
        "onesr": np.ones((1, 128), dtype=np.float32).astype(bf16),
    }


def _shard_tokens(x):
    return [np.ascontiguousarray(
        np.asarray(x)[:, SPC * c:SPC * (c + 1), :].reshape(TPC, DM))
        for c in range(NCORE)]


def _unshard_tokens(parts):
    x = np.empty((B, S, DM), dtype=np.float32)
    for c, part in enumerate(parts):
        x[:, SPC * c:SPC * (c + 1), :] = np.asarray(part).reshape(B, SPC, DM)
    return x


def run_l0_ffn(x_full, w_router, w1, b1, w2, b2, n_experts=8, halves=2):
    nc = build_l0_ffn(n_experts, halves)
    shards = _shard_tokens(x_full)
    maps = [_l0ffn_inputs(s, w_router, w1, b1, w2, b2) for s in shards]
    res = run_bass_kernel_spmd(nc, maps, core_ids=list(range(NCORE)))
    return _unshard_tokens([r["y"] for r in res.results])



def emit_tp_tile(p, src_fn, dst_fn, nblk, ps_bf, ident, bank, after=()):
    toks = []
    tt = None
    for i in range(nblk):
        blk = i % 4
        last = (i == nblk - 1)
        tt = p.on("T", lambda E, i=i, blk=blk: E.transpose(
            ps_bf[:, 128 * blk:128 * blk + 128], src_fn(i), ident[:, :]),
            after=tuple(after) + p.pw(bank),
            inc=(1 if (blk == 3 or last) else 0))
        if blk == 3 or last:
            lo = i - blk
            te = p.on("S", lambda E, lo=lo, i=i: E.activation(
                dst_fn(lo, i + 1), ps_bf[:, 0:128 * (i + 1 - lo)], AF.Identity),
                after=(tt,), inc=1)
            p.pset([bank], te)
            toks.append(te)
    return toks, toks[-1]


def emit_proj_tm(p, lhs_fn, nj, w_sb, nout, ps, banks, bias_row=None,
                 onesr=None, after=(), nmax=512):
    nslc = (nout + nmax - 1) // nmax
    waits = tuple(after) + p.pw(*banks)
    tok = None
    for sl in range(nslc):
        lo, hi = sl * nmax, min(nout, (sl + 1) * nmax)
        if bias_row is not None:
            p.on("T", lambda E, lo=lo, hi=hi: E.matmul(
                ps[:, lo:hi], onesr[:, :], bias_row[:, lo:hi],
                start=True, stop=False), after=waits, inc=0)
        for j in range(nj):
            st = (bias_row is None and j == 0)
            sp = (j == nj - 1)
            tok = p.on("T", lambda E, j=j, lo=lo, hi=hi, st=st, sp=sp: E.matmul(
                ps[:, lo:hi], lhs_fn(j), w_sb[:, j, lo:hi], start=st, stop=sp),
                after=waits, inc=(1 if sp else 0))
    return tok


def build_l0_attn_b():
    p = Prog("l0attnb")
    x_h = p.din("x", [TPC, DM])
    wq_h = p.din("wq", [128, 8, DM])
    wdn_h = p.din("wdn", [128, 8, 256])
    wup_h = p.din("wup", [128, 2, 2048])
    wo_h = p.din("wo", [128, 8, DM])
    bq_h = p.din("bq", [1, DM])
    bdn_h = p.din("bdn", [1, 256])
    bup_h = p.din("bup", [1, 2048])
    bo_h = p.din("bo", [1, DM])
    cs_h = p.din("ropecos", [128, 4, 32])
    sn_h = p.din("ropesin", [128, 4, 32])
    id_h = p.din("ident", [128, 128], BF16)
    on_h = p.din("onesr", [1, 128], BF16)
    y_h = p.dout("y", [TPC, DM])

    NT = 16
    xg = p.sb([128, 2, DM], F32, "xg")
    ybf = p.sb([128, 2, DM], BF16, "ybf")
    yfm = p.sb([128, 8, 128], BF16, "yfm")
    q_tm = p.sb([128, DM], F32, "qtm")
    qr = p.sb([128, DM], F32, "qr")
    kvl_bf = p.sb([128, 256], BF16, "kvl")
    kvl_fm = p.sb([128, 2, 128], BF16, "kvlfm")
    kv_tm = p.sb([128, 2048], F32, "kvtm")
    kr = p.sb([128, DM], F32, "kr")
    att = p.sb([128, 256], F32, "att")
    mx16 = p.sb([128, 64], F32, "mx16")
    acc = p.sb([128, DM], F32, "acc")
    av_bf = p.sb([128, DM], BF16, "avbf")
    avfm = p.sb([128, 8, 128], BF16, "avfm")
    tmp_v = p.sb([128, DM], F32, "tmpv")
    tmp_p = p.sb([128, DM], F32, "tmpp")
    cent = p.sb([128, DM], F32, "cent")
    sq = p.sb([128, DM], F32, "sq")
    col4 = p.sb([128, 8], F32, "col4")
    wq = p.sb([128, 8, DM], BF16, "wq")
    wdn = p.sb([128, 8, 256], BF16, "wdn")
    wup = p.sb([128, 2, 2048], BF16, "wup")
    wo = p.sb([128, 8, DM], BF16, "wo")
    bq = p.sb([1, DM], BF16, "bq")
    bdn = p.sb([1, 256], BF16, "bdn")
    bup = p.sb([1, 2048], BF16, "bup")
    bo = p.sb([1, DM], BF16, "bo")
    cs = p.sb([128, 4, 32], F32, "cs")
    sn = p.sb([128, 4, 32], F32, "sn")
    ident = p.sb([128, 128], BF16, "ident")
    onesr = p.sb([1, 128], BF16, "onesr")

    pall = p.psum([128, 4096], F32, "pall")
    tp_ps = pall[:, 0:256].bitcast(BF16)

    sc = {"cent": cent, "sq": sq, "col4": col4}

    tk_w = [p.dma("P", wq[:, :, :], wq_h[:, :, :]),
            p.dma("P", wdn[:, :, :], wdn_h[:, :, :]),
            p.dma("P", wup[:, :, :], wup_h[:, :, :]),
            p.dma("P", wo[:, :, :], wo_h[:, :, :]),
            p.dma("P", bq[:, :], bq_h[:, :]),
            p.dma("P", bdn[:, :], bdn_h[:, :]),
            p.dma("P", bup[:, :], bup_h[:, :]),
            p.dma("P", bo[:, :], bo_h[:, :]),
            p.dma("Q", cs[:, :, :], cs_h[:, :, :]),
            p.dma("Q", sn[:, :, :], sn_h[:, :, :]),
            p.dma("Q", ident[:, :], id_h[:, :]),
            p.dma("Q", onesr[:, :], on_h[:, :])]
    tk_w = tuple(tk_w)

    store_toks = []
    xbuf_free = [None, None]
    scratch_free = {k: None for k in
                    ["ybf", "yfm", "qtm", "qr", "kvl", "kvlfm", "kvtm", "kr",
                     "att", "acc", "avbf", "avfm", "tmpv", "tmpp", "mx16"]}

    for t in range(NT):
        u = t % 4
        bi = t % 2
        src = bass.AP(x_h.tensor, t * 128 * DM, [[DM, 128], [1, DM]])
        tk_x = p.dma("Q", xg[:, bi, :], src, after=(xbuf_free[bi],))

        ty, _ = emit_ln_tile(p, xg[:, bi, :], ybf[:, bi, :], sc, t % 2,
                             after=(tk_x, scratch_free["ybf"]),
                             prev_cent_reader=scratch_free["ybf"])
        tp_toks, tp_last = emit_tp_tile(
            p, lambda i, bi=bi: ybf[:, bi, 128 * i:128 * i + 128],
            lambda lo, hi: yfm[:, lo:hi, :].rearrange("p a b -> p (a b)"),
            8, tp_ps, ident, 0,
            after=(ty, tk_w[10], scratch_free["yfm"]))

        qmm = emit_proj_tm(p, lambda j: yfm[:, j, :], 8, wq, DM,
                           pall[:, 512:1536], (1, 2), bias_row=bq, onesr=onesr,
                           after=(tp_last,) + tk_w[:1] + (tk_w[4], tk_w[11],
                                                          scratch_free["qtm"]))
        qe = p.on("S", lambda E: E.activation(q_tm[:, :], pall[:, 512:1536],
                                              AF.Identity), after=(qmm,), inc=1)
        p.pset([1, 2], qe)

        lmm = emit_proj_tm(p, lambda j: yfm[:, j, :], 8, wdn, 256,
                           pall[:, 1536:1792], (3,), bias_row=bdn, onesr=onesr,
                           after=(tp_last, tk_w[1], tk_w[5],
                                  scratch_free["kvl"]))
        le = p.on("S", lambda E: E.activation(kvl_bf[:, :], pall[:, 1536:1792],
                                              AF.Identity), after=(lmm,), inc=1)
        p.pset([3], le)
        ltp, ltp_last = emit_tp_tile(
            p, lambda i: kvl_bf[:, 128 * i:128 * i + 128],
            lambda lo, hi: kvl_fm[:, lo:hi, :].rearrange("p a b -> p (a b)"),
            2, tp_ps, ident, 0, after=(le, scratch_free["kvlfm"]))

        kmm = emit_proj_tm(p, lambda j: kvl_fm[:, j, :], 2, wup, 2048,
                           pall[:, 2048:4096], (4, 5, 6, 7), bias_row=bup,
                           onesr=onesr,
                           after=(ltp_last, tk_w[2], tk_w[6],
                                  scratch_free["kvtm"]))
        ke = p.on("S", lambda E: E.activation(kv_tm[:, :], pall[:, 2048:4096],
                                              AF.Identity), after=(kmm,), inc=1)
        p.pset([4, 5, 6, 7], ke)

        def rope(eng, src3, dst3, tmp3, after):
            csr = cs[:, u, :].unsqueeze(1).to_broadcast([128, 16, 32])
            snr = sn[:, u, :].unsqueeze(1).to_broadcast([128, 16, 32])
            x1 = src3[0]; x2 = src3[1]; o1 = dst3[0]; o2 = dst3[1]
            a1 = p.on(eng, lambda E: E.tensor_tensor(o1, x1, csr, ALU.mult),
                      after=after, inc=1)
            a2 = p.on(eng, lambda E: E.tensor_tensor(tmp3, x2, snr, ALU.mult),
                      after=after, inc=1)
            a3 = p.on(eng, lambda E: E.tensor_sub(o1, o1, tmp3),
                      after=(a1, a2), inc=1)
            a4 = p.on(eng, lambda E: E.tensor_tensor(tmp3, x1, snr, ALU.mult),
                      after=(a3,), inc=1)
            a5 = p.on(eng, lambda E: E.tensor_tensor(o2, x2, csr, ALU.mult),
                      after=(a4,), inc=1)
            a6 = p.on(eng, lambda E: E.tensor_add(o2, o2, tmp3),
                      after=(a5,), inc=1)
            return a6

        q3 = q_tm[:, :].rearrange("p (h d) -> p h d", h=16)
        qr3 = qr[:, :].rearrange("p (h d) -> p h d", h=16)
        tv3 = tmp_v[:, 0:512].rearrange("p (h d) -> p h d", h=16)
        rq = rope("V", (q3[:, :, 0:32], q3[:, :, 32:64]),
                  (qr3[:, :, 0:32], qr3[:, :, 32:64]), tv3,
                  (qe, tk_w[8], tk_w[9], scratch_free["qr"]))
        kv3 = kv_tm[:, :].rearrange("p (h d) -> p h d", h=16)
        kr3 = kr[:, :].rearrange("p (h dd) -> p h dd", h=16)
        tp3 = tmp_p[:, 0:512].rearrange("p (h d) -> p h d", h=16)
        rk = rope("P", (kv3[:, :, 0:32], kv3[:, :, 32:64]),
                  (kr3[:, :, 0:32], kr3[:, :, 32:64]), tp3,
                  (ke, tk_w[8], tk_w[9], scratch_free["kr"]))

        at3 = att[:, :].rearrange("p (h g) -> p h g", g=16)
        sc_toks = []
        prev_mult = scratch_free["att"]
        for g in range(16):
            krg = kr3[:, g, :].unsqueeze(1).to_broadcast([128, 16, 64])
            m1 = p.on("P", lambda E, krg=krg: E.tensor_tensor(
                tp3.to_broadcast([128, 16, 32]) if False else
                tmp_p[:, :].rearrange("p (h d) -> p h d", h=16),
                qr3, krg, ALU.mult), after=(rq, rk, prev_mult), inc=1)
            m2 = p.on("V", lambda E, g=g: E.tensor_reduce(
                at3[:, :, g:g + 1], tmp_p[:, :].rearrange(
                    "p (h d) -> p h d", h=16), AX.X, ALU.add),
                after=(m1,), inc=1)
            prev_mult = m2
            sc_toks.append(m2)
        mxc = mx16[:, 0:16]
        smc = mx16[:, 16:32]
        rdc = mx16[:, 32:48]
        s1 = p.on("V", lambda E: E.tensor_reduce(
            mxc.unsqueeze(2), at3, AX.X, ALU.max_, negate=False)
            if False else E.reduce_max(mxc.unsqueeze(2), at3, AX.X),
            after=tuple(sc_toks) + (scratch_free["mx16"],), inc=1)
        s2 = p.on("V", lambda E: E.tensor_tensor(
            at3, at3, mxc.unsqueeze(2).to_broadcast([128, 16, 16]), ALU.subtract),
            after=(s1,), inc=1)
        s3 = p.on("S", lambda E: E.activation(att[:, :], att[:, :], AF.Exp,
                                              scale=0.125), after=(s2,), inc=1)
        s4 = p.on("V", lambda E: E.tensor_reduce(
            smc.unsqueeze(2), at3, AX.X, ALU.add), after=(s3,), inc=1)
        s5 = p.on("V", lambda E: E.reciprocal(rdc, smc), after=(s4,), inc=1)

        acc3 = acc[:, :].rearrange("p (h d) -> p h d", h=16)
        cb = None
        for g in range(16):
            eg = at3[:, :, g:g + 1].to_broadcast([128, 16, 64])
            vg = kv3[:, g, 64:128].unsqueeze(1).to_broadcast([128, 16, 64])
            if g == 0:
                cb = p.on("P", lambda E, eg=eg, vg=vg: E.tensor_tensor(
                    acc3, eg, vg, ALU.mult),
                    after=(s3, scratch_free["acc"], prev_mult), inc=1)
            else:
                c1 = p.on("P", lambda E, eg=eg, vg=vg: E.tensor_tensor(
                    tmp_p[:, :].rearrange("p (h d) -> p h d", h=16), eg, vg,
                    ALU.mult), after=(cb,), inc=1)
                cb = p.on("P", lambda E: E.tensor_add(
                    acc[:, :], acc[:, :], tmp_p[:, :]), after=(c1,), inc=1)
        av = p.on("V", lambda E: E.tensor_tensor(
            av_bf[:, :].rearrange("p (h d) -> p h d", h=16), acc3,
            rdc.unsqueeze(2).to_broadcast([128, 16, 64]), ALU.mult),
            after=(cb, s5, scratch_free["avbf"]), inc=1)

        otp, otp_last = emit_tp_tile(
            p, lambda i: av_bf[:, 128 * i:128 * i + 128],
            lambda lo, hi: avfm[:, lo:hi, :].rearrange("p a b -> p (a b)"),
            8, tp_ps, ident, 0, after=(av, scratch_free["avfm"]))
        omm = emit_proj_tm(p, lambda j: avfm[:, j, :], 8, wo, DM,
                           pall[:, 512:1536], (1, 2), bias_row=bo, onesr=onesr,
                           after=(otp_last, tk_w[3], tk_w[7]))
        orr = p.on("V", lambda E, bi=bi: E.tensor_tensor(
            xg[:, bi, :], pall[:, 512:1536], xg[:, bi, :], ALU.add),
            after=(omm,), inc=1)
        p.pset([1, 2], orr)

        dst = bass.AP(y_h.tensor, t * 128 * DM, [[DM, 128], [1, DM]])
        st = p.dma("Q", dst, xg[:, bi, :], after=(orr,))
        store_toks.append(st)
        xbuf_free[bi] = st

        scratch_free["ybf"] = ty if False else tp_last
        scratch_free["yfm"] = qmm if False else lmm
        scratch_free["yfm"] = lmm
        scratch_free["qtm"] = rq
        scratch_free["qr"] = prev_mult
        scratch_free["kvl"] = ltp_last
        scratch_free["kvlfm"] = kmm
        scratch_free["kvtm"] = cb
        scratch_free["kr"] = prev_mult
        scratch_free["att"] = av
        scratch_free["acc"] = av
        scratch_free["avbf"] = omm
        scratch_free["avfm"] = omm
        scratch_free["tmpv"] = rq
        scratch_free["tmpp"] = cb
        scratch_free["mx16"] = av

    return p.finalize(store_toks)


def _rope_tables(core):
    pos = np.arange(SPC * core, SPC * (core + 1), dtype=np.float64)
    inv = 1.0 / 10000.0 ** (np.arange(0, 64, 2, dtype=np.float64) / 64.0)
    fr = pos[:, None] * inv[None, :]
    cs = np.cos(fr).astype(np.float32).reshape(4, 128, 32).transpose(1, 0, 2)
    sn = np.sin(fr).astype(np.float32).reshape(4, 128, 32).transpose(1, 0, 2)
    return np.ascontiguousarray(cs), np.ascontiguousarray(sn)


def _l0attnb_inputs(core, x_own, Wq, bq_, Wdown, bdown, Wup, bup_, Wout, bout):
    cs_np, sn_np = _rope_tables(core)
    r = lambda w, nj, no: np.ascontiguousarray(
        np.asarray(w).reshape(nj, 128, no).transpose(1, 0, 2))
    return {
        "x": np.ascontiguousarray(x_own, dtype=np.float32),
        "wq": r(Wq, 8, DM), "wdn": r(Wdown, 8, 256), "wup": r(Wup, 2, 2048),
        "wo": r(Wout, 8, DM),
        "bq": np.asarray(bq_).reshape(1, DM).astype(np.float32),
        "bdn": np.asarray(bdown).reshape(1, 256).astype(np.float32),
        "bup": np.asarray(bup_).reshape(1, 2048).astype(np.float32),
        "bo": np.asarray(bout).reshape(1, DM).astype(np.float32),
        "ropecos": cs_np, "ropesin": sn_np,
        "ident": np.eye(128, dtype=np.float32).astype(bf16),
        "onesr": np.ones((1, 128), dtype=np.float32).astype(bf16),
    }


def run_l0_attn_b(x_full, Wq, bq_, Wdown, bdown, Wup, bup_, Wout, bout,
                  cores=None):
    nc = build_l0_attn_b()
    shards = _shard_tokens(x_full)
    cores = cores or list(range(NCORE))
    maps = [_l0attnb_inputs(c, shards[c], Wq, bq_, Wdown, bdown, Wup, bup_,
                            Wout, bout) for c in cores]
    res = run_bass_kernel_spmd(nc, maps, core_ids=cores)
    if len(cores) == NCORE:
        return _unshard_tokens([r["y"] for r in res.results])
    return [np.asarray(r["y"]) for r in res.results]



def build_l1_attn_a():
    p = Prog("l1attna")
    x_h = p.din("x", [TPC, DM])
    wq_h = p.din("wq", [128, 8, DM])
    wk_h = p.din("wk", [128, 8, DM])
    wv_h = p.din("wv", [128, 8, DM])
    wf_h = p.din("wf", [128, 8, 16])
    wo_h = p.din("wo", [128, 8, DM])
    bq_h = p.din("bq", [1, DM]); bk_h = p.din("bk", [1, DM])
    bv_h = p.din("bv", [1, DM]); bf_h = p.din("bf", [1, 16])
    bo_h = p.din("bo", [1, DM])
    id_h = p.din("ident", [128, 128], BF16)
    on_h = p.din("onesr", [1, 128], BF16)
    y_h = p.dout("y", [TPC, DM])

    xg = p.sb([128, 4, DM], F32, "xg")
    ybf = p.sb([128, DM], BF16, "ybf")
    yfm = p.sb([128, 8, 128], BF16, "yfm")
    pq = p.sb([128, 4, DM], F32, "pq")
    pk = p.sb([128, 4, DM], F32, "pk")
    vv = p.sb([128, 4, DM], F32, "vv")
    ff = p.sb([128, 4, 16], F32, "ff")
    alpha = p.sb([128, 10, 16], F32, "alpha")
    coefs = p.sb([128, 10, 16], F32, "coefs")
    dcol = p.sb([128, 8, 16], F32, "dcol")
    acc = p.sb([128, DM], F32, "acc")
    ob = p.sb([128, DM], BF16, "ob")
    ofm = p.sb([128, 8, 128], BF16, "ofm")
    tmp_p = p.sb([128, DM], F32, "tmpp")
    tmp_v = p.sb([128, DM], F32, "tmpv")
    cent = p.sb([128, DM], F32, "cent")
    sq = p.sb([128, DM], F32, "sq")
    col4 = p.sb([128, 8], F32, "col4")
    wq = p.sb([128, 8, DM], BF16, "wq")
    wk = p.sb([128, 8, DM], BF16, "wk")
    wv = p.sb([128, 8, DM], BF16, "wv")
    wf = p.sb([128, 8, 16], BF16, "wf")
    wo = p.sb([128, 8, DM], BF16, "wo")
    bq = p.sb([1, DM], BF16, "bq"); bk = p.sb([1, DM], BF16, "bk")
    bv = p.sb([1, DM], BF16, "bv"); bfb = p.sb([1, 16], BF16, "bf")
    bo = p.sb([1, DM], BF16, "bo")
    ident = p.sb([128, 128], BF16, "ident")
    onesr = p.sb([1, 128], BF16, "onesr")

    pall = p.psum([128, 4096], F32, "pall")
    tp_ps = pall[:, 0:256].bitcast(BF16)

    sc = {"cent": cent, "sq": sq, "col4": col4}
    PAIRS = [(i, j) for i in range(4) for j in range(i + 1)]
    PIDX = {pr: n for n, pr in enumerate(PAIRS)}

    tk_w = tuple([
        p.dma("P", wq[:, :, :], wq_h[:, :, :]),
        p.dma("P", wk[:, :, :], wk_h[:, :, :]),
        p.dma("P", wv[:, :, :], wv_h[:, :, :]),
        p.dma("P", wf[:, :, :], wf_h[:, :, :]),
        p.dma("P", wo[:, :, :], wo_h[:, :, :]),
        p.dma("P", bq[:, :], bq_h[:, :]),
        p.dma("P", bk[:, :], bk_h[:, :]),
        p.dma("P", bv[:, :], bv_h[:, :]),
        p.dma("P", bfb[:, :], bf_h[:, :]),
        p.dma("P", bo[:, :], bo_h[:, :]),
        p.dma("Q", ident[:, :], id_h[:, :]),
        p.dma("Q", onesr[:, :], on_h[:, :])])

    store_toks = []
    fr = {k: None for k in ["xg", "ybf", "yfm", "pq", "pk", "vv", "ff",
                            "alpha", "coefs", "acc", "ob", "ofm",
                            "tmpp", "tmpv", "dcol"]}

    def phi_from_psum(ps_ap, dst_ap, tmp_ap, after):
        a1 = p.on("V", lambda E: E.tensor_scalar_min(tmp_ap, ps_ap, 0.0),
                  after=after, inc=1)
        a2 = p.on("S", lambda E: E.activation(dst_ap, tmp_ap, AF.Exp),
                  after=(a1,), inc=1)
        a3 = p.on("S", lambda E: E.activation(tmp_ap, ps_ap, AF.Relu),
                  after=(a2,), inc=1)
        a4 = p.on("V", lambda E: E.tensor_add(dst_ap, dst_ap, tmp_ap),
                  after=(a3,), inc=1)
        return a4

    for u in range(4):
        ph_toks = []
        for b in range(4):
            t = b * 4 + u
            src = bass.AP(x_h.tensor, t * 128 * DM, [[DM, 128], [1, DM]])
            tk_x = p.dma("Q", xg[:, b, :], src, after=(fr["xg"],))
            ty, _ = emit_ln_tile(p, xg[:, b, :], ybf[:, :], sc, b % 2,
                                 after=(tk_x, fr["ybf"]),
                                 prev_cent_reader=fr["ybf"])
            tp_toks, tp_last = emit_tp_tile(
                p, lambda i: ybf[:, 128 * i:128 * i + 128],
                lambda lo, hi: yfm[:, lo:hi, :].rearrange("p a b -> p (a b)"),
                8, tp_ps, ident, 0, after=(ty, tk_w[10], fr["yfm"]))
            fr["ybf"] = tp_last

            qmm = emit_proj_tm(p, lambda j: yfm[:, j, :], 8, wq, DM,
                               pall[:, 512:1536], (1, 2), bias_row=bq,
                               onesr=onesr,
                               after=(tp_last, tk_w[0], tk_w[5], tk_w[11],
                                      fr["pq"]))
            tq = phi_from_psum(pall[:, 512:1536], pq[:, b, :], tmp_v[:, :],
                               (qmm, fr["tmpv"]))
            p.pset([1, 2], tq)
            fr["tmpv"] = tq
            kmm = emit_proj_tm(p, lambda j: yfm[:, j, :], 8, wk, DM,
                               pall[:, 1536:2560], (3, 4), bias_row=bk,
                               onesr=onesr,
                               after=(tp_last, tk_w[1], tk_w[6], fr["pk"]))
            tk2 = phi_from_psum(pall[:, 1536:2560], pk[:, b, :], tmp_p[:, :],
                                (kmm, fr["tmpp"]))
            p.pset([3, 4], tk2)
            fr["tmpp"] = tk2
            vmm = emit_proj_tm(p, lambda j: yfm[:, j, :], 8, wv, DM,
                               pall[:, 2560:3584], (5, 6), bias_row=bv,
                               onesr=onesr,
                               after=(tp_last, tk_w[2], tk_w[7], fr["vv"]))
            ve = p.on("S", lambda E, b=b: E.activation(
                vv[:, b, :], pall[:, 2560:3584], AF.Identity),
                after=(vmm,), inc=1)
            p.pset([5, 6], ve)
            fmm = emit_proj_tm(p, lambda j: yfm[:, j, :], 8, wf, 16,
                               pall[:, 3584:3600], (7,), bias_row=bfb,
                               onesr=onesr,
                               after=(tp_last, tk_w[3], tk_w[8], fr["ff"]))
            f1 = p.on("S", lambda E, b=b: E.activation(
                ff[:, b, :], pall[:, 3584:3600], AF.Exp, scale=-1.0),
                after=(fmm,), inc=1)
            p.pset([7], f1)
            f2 = p.on("V", lambda E, b=b: E.tensor_scalar(
                ff[:, b, :], ff[:, b, :], 1.0, None, ALU.add), after=(f1,), inc=1)
            f3 = p.on("V", lambda E, b=b: E.reciprocal(ff[:, b, :], ff[:, b, :]),
                      after=(f2,), inc=1)
            fr["yfm"] = fmm
            ph_toks += [tq, tk2, ve, f3]
        ph_done = tuple(ph_toks)

        al_toks = {}
        prevm = fr["alpha"]
        for (i, j) in PAIRS:
            m1 = p.on("P", lambda E, i=i, j=j: E.tensor_tensor(
                tmp_p[:, :].rearrange("p (h d) -> p h d", h=16),
                pq[:, i, :].rearrange("p (h d) -> p h d", h=16),
                pk[:, j, :].rearrange("p (h d) -> p h d", h=16), ALU.mult),
                after=ph_done + (prevm, fr["tmpp"]), inc=1)
            m2 = p.on("V", lambda E, i=i, j=j: E.tensor_reduce(
                alpha[:, PIDX[(i, j)], :].unsqueeze(2),
                tmp_p[:, :].rearrange("p (h d) -> p h d", h=16), AX.X, ALU.add),
                after=(m1,), inc=1)
            prevm = m2
            al_toks[(i, j)] = m2
        fr["tmpp"] = prevm

        cf_toks = {}
        rd_toks = {}
        last_cf = fr["coefs"]
        for i in range(4):
            den = dcol[:, i, :]
            Fch = dcol[:, 4 + i % 4, :]
            c0 = p.on("V", lambda E, i=i: E.tensor_copy(
                coefs[:, PIDX[(i, i)], :], alpha[:, PIDX[(i, i)], :]),
                after=(al_toks[(i, i)], last_cf, fr["dcol"]), inc=1)
            cf_toks[(i, i)] = c0
            dsum = p.on("V", lambda E, i=i: E.tensor_scalar(
                dcol[:, i, :], alpha[:, PIDX[(i, i)], :], 1e-6, None, ALU.add),
                after=(c0,), inc=1)
            prev_f = None
            for j in range(i - 1, -1, -1):
                if j == i - 1:
                    pf = p.on("V", lambda E, i=i, j=j: E.tensor_copy(
                        Fch, ff[:, j + 1, :]), after=(dsum,), inc=1)
                else:
                    pf = p.on("V", lambda E, i=i, j=j: E.tensor_tensor(
                        Fch, Fch, ff[:, j + 1, :], ALU.mult),
                        after=(prev_f,), inc=1)
                prev_f = pf
                cc = p.on("V", lambda E, i=i, j=j: E.tensor_tensor(
                    coefs[:, PIDX[(i, j)], :], alpha[:, PIDX[(i, j)], :], Fch,
                    ALU.mult), after=(pf, al_toks[(i, j)]), inc=1)
                cf_toks[(i, j)] = cc
                prev_f = p.on("V", lambda E, i=i, j=j: E.tensor_tensor(
                    dcol[:, i, :], dcol[:, i, :], coefs[:, PIDX[(i, j)], :],
                    ALU.add), after=(cc,), inc=1)
            rd = p.on("V", lambda E, i=i: E.reciprocal(
                dcol[:, i, :], dcol[:, i, :]),
                after=(prev_f if prev_f else dsum,), inc=1)
            rd_toks[i] = rd
            last_cf = rd
        fr["dcol"] = last_cf
        fr["alpha"] = last_cf

        for i in range(4):
            t = i * 4 + u
            cb = None
            for nj, j in enumerate(range(i, -1, -1)):
                wt_ = p.on("V", lambda E, i=i, j=j: E.tensor_tensor(
                    coefs[:, PIDX[(i, j)], :], coefs[:, PIDX[(i, j)], :],
                    dcol[:, i, :], ALU.mult),
                    after=(cf_toks[(i, j)], rd_toks[i]), inc=1)
                eg = coefs[:, PIDX[(i, j)], :].unsqueeze(2).to_broadcast(
                    [128, 16, 64])
                vj = vv[:, j, :].rearrange("p (h d) -> p h d", h=16)
                if nj == 0:
                    cb = p.on("P", lambda E, eg=eg, vj=vj: E.tensor_tensor(
                        acc[:, :].rearrange("p (h d) -> p h d", h=16), eg, vj,
                        ALU.mult), after=(wt_, fr["acc"], fr["tmpp"]), inc=1)
                else:
                    c1 = p.on("P", lambda E, eg=eg, vj=vj: E.tensor_tensor(
                        tmp_p[:, :].rearrange("p (h d) -> p h d", h=16), eg, vj,
                        ALU.mult), after=(wt_, cb), inc=1)
                    cb = p.on("P", lambda E: E.tensor_add(
                        acc[:, :], acc[:, :], tmp_p[:, :]), after=(c1,), inc=1)
            fr["tmpp"] = cb
            ocast = p.on("V", lambda E: E.tensor_copy(ob[:, :], acc[:, :]),
                         after=(cb, fr["ob"]), inc=1)
            fr["acc"] = ocast
            otp, otp_last = emit_tp_tile(
                p, lambda ii: ob[:, 128 * ii:128 * ii + 128],
                lambda lo, hi: ofm[:, lo:hi, :].rearrange("p a b -> p (a b)"),
                8, tp_ps, ident, 0, after=(ocast, fr["ofm"]))
            omm = emit_proj_tm(p, lambda j: ofm[:, j, :], 8, wo, DM,
                               pall[:, 512:1536], (1, 2), bias_row=bo,
                               onesr=onesr,
                               after=(otp_last, tk_w[4], tk_w[9]))
            orr = p.on("V", lambda E, i=i: E.tensor_tensor(
                xg[:, i, :], pall[:, 512:1536], xg[:, i, :], ALU.add),
                after=(omm,), inc=1)
            p.pset([1, 2], orr)
            fr["ob"] = otp_last
            fr["ofm"] = omm
            dst = bass.AP(y_h.tensor, t * 128 * DM, [[DM, 128], [1, DM]])
            st = p.dma("Q", dst, xg[:, i, :], after=(orr,))
            store_toks.append(st)
        fr["xg"] = p.barrier_token("Q", tuple(store_toks[-4:]))
        fr["pq"] = fr["tmpp"]
        fr["pk"] = fr["tmpp"]
        fr["vv"] = fr["tmpp"]
        fr["ff"] = last_cf
        fr["coefs"] = fr["tmpp"]

    return p.finalize(store_toks)


def _l1attna_inputs(x_own, Wq, bq_, Wk, bk_, Wv, bv_, Wf, bf_, Wo, bo_):
    r = lambda w, no: np.ascontiguousarray(
        np.asarray(w).T.reshape(8, 128, no).transpose(1, 0, 2))
    return {
        "x": np.ascontiguousarray(x_own, dtype=np.float32),
        "wq": r(Wq, DM), "wk": r(Wk, DM), "wv": r(Wv, DM), "wf": r(Wf, 16),
        "wo": r(Wo, DM),
        "bq": np.asarray(bq_).reshape(1, DM).astype(np.float32),
        "bk": np.asarray(bk_).reshape(1, DM).astype(np.float32),
        "bv": np.asarray(bv_).reshape(1, DM).astype(np.float32),
        "bf": np.asarray(bf_).reshape(1, 16).astype(np.float32),
        "bo": np.asarray(bo_).reshape(1, DM).astype(np.float32),
        "ident": np.eye(128, dtype=np.float32).astype(bf16),
        "onesr": np.ones((1, 128), dtype=np.float32).astype(bf16),
    }


def run_l1_attn_a(x_full, *args, cores=None):
    nc = build_l1_attn_a()
    shards = _shard_tokens(x_full)
    cores = cores or list(range(NCORE))
    maps = [_l1attna_inputs(shards[c], *args) for c in cores]
    res = run_bass_kernel_spmd(nc, maps, core_ids=cores)
    if len(cores) == NCORE:
        return _unshard_tokens([r["y"] for r in res.results])
    return [np.asarray(r["y"]) for r in res.results]



def build_l1ab_kv(ys=False):
    p = Prog("l0aakv" if ys else "l1abkv")
    xo_h = p.din("xown", [TPC, DM])
    wk_h = p.din("wk", [128, 8, DM])
    wv_h = p.din("wv", [128, 8, DM])
    bkf_h = p.din("bkf", [128, 8])
    bvf_h = p.din("bvf", [128, 8])
    id_h = p.din("ident", [128, 128], BF16)
    k_out = p.dout("kout", [DM, TPC], BF16)
    v_out = p.dout("vout", [TPC, DM], BF16)
    y_out = p.dout("yout", [128, 8, TPC], BF16)
    if ys:
        ys_out = p.dout("ysout", [128, 32])
        ysum = p.sb([128, 8, 4], F32, "ysum")
        yrtmp = p.sb([128, 4], F32, "yrtmp")
        ysinit = p.on("V", lambda E: E.memset(ysum[:, :, :], 0.0),
                      after=(), inc=1)
        ys_prev = ysinit

    xg = p.sb([128, 4, DM], F32, "xg")
    ybf = p.sb([128, 4, DM], BF16, "ybf")
    yfm = p.sb([128, 8, 512], BF16, "yfm")
    kfm = p.sb([128, 8, 512], BF16, "kfm")
    vfm = p.sb([128, 8, 512], BF16, "vfm")
    vtm = p.sb([128, 4, DM], BF16, "vtm")
    scr = p.sb([128, 2, DM], F32, "scr")
    col4 = p.sb([128, 16], F32, "col4")
    wab = p.sb([128, 8, DM], BF16, "wab")
    wcd = p.sb([128, 8, DM], BF16, "wcd")
    bkf = p.sb([128, 8], F32, "bkf")
    bvf = p.sb([128, 8], F32, "bvf")
    ident = p.sb([128, 128], BF16, "ident")
    pall = p.psum([128, 4096], F32, "pall")
    tp_ps = pall[:, 0:256].bitcast(BF16)
    sc = {"cent": scr[:, 0, :], "sq": scr[:, 1, :], "col4": col4}

    tk = {}
    tk["wk"] = p.dma("P", wab[:, :, :], wk_h[:, :, :])
    tk["wv"] = p.dma("P", wcd[:, :, :], wv_h[:, :, :])
    tk["bkf"] = p.dma("Q", bkf[:, :], bkf_h[:, :])
    tk["bvf"] = p.dma("Q", bvf[:, :], bvf_h[:, :])
    tk["ident"] = p.dma("Q", ident[:, :], id_h[:, :])

    fr = {}
    spill_toks = []
    for g in range(TPC // 512):
        srcx = bass.AP(xo_h.tensor, g * 512 * DM,
                       [[DM, 128], [128 * DM, 4], [1, DM]])
        tk_x = p.dma("Q", xg[:, :, :], srcx, after=(fr.get("xg"),))
        ln_toks = []
        for t in range(4):
            ty, _ = emit_ln_tile(p, xg[:, t, :], ybf[:, t, :], sc, t,
                                 after=(tk_x, fr.get("ybf")),
                                 prev_cent_reader=fr.get("ybf"))
            ln_toks.append(ty)
        tp_toks = []
        free_tok = fr.get("yfm")
        for j in range(8):
            for t in range(4):
                blk = t % 4
                tt = p.on("T", lambda E, j=j, t=t, blk=blk: E.transpose(
                    tp_ps[:, 128 * blk:128 * blk + 128],
                    ybf[:, t, 128 * j:128 * j + 128], ident[:, :]),
                    after=(ln_toks[t], tk["ident"]) + p.pw(0),
                    inc=(1 if blk == 3 else 0))
                if blk == 3:
                    te = p.on("S", lambda E, j=j: E.activation(
                        yfm[:, j, :], tp_ps, AF.Identity),
                        after=(tt, free_tok), inc=1)
                    free_tok = None
                    p.pset([0], te)
                    tp_toks.append(te)
        tp_last = tp_toks[-1]
        fr["ybf"] = tp_last
        fr["xg"] = tp_last
        dsty = bass.AP(y_out.tensor, g * 512,
                       [[8 * TPC, 128], [TPC, 8], [1, 512]])
        yst = p.dma("Q", dsty, yfm[:, :, :], after=(tp_last,))
        spill_toks.append(yst)
        if ys:
            bb = g // (TPC // 512 // B) if False else g % B
            for j in range(8):
                r1 = p.on("V", lambda E, j=j: E.tensor_reduce(
                    yrtmp[:, 0:1], yfm[:, j, :], AX.X, ALU.add),
                    after=(tp_last, ys_prev), inc=1)
                ys_prev = p.on("V", lambda E, j=j, bb=bb: E.tensor_add(
                    ysum[:, j, bb:bb + 1], ysum[:, j, bb:bb + 1],
                    yrtmp[:, 0:1]), after=(r1,), inc=1)

        ev = None
        for (wsb, bfm, tkw, tkb, dst, banks, frk) in [
                (wab, bkf, tk["wk"], tk["bkf"], kfm, (1, 2), "kfm"),
                (wcd, bvf, tk["wv"], tk["bvf"], vfm, (3, 4), "vfm")]:
            for oc in range(8):
                bank = banks[oc % 2]
                ps_oc = pall[:, 512 * bank:512 * (bank + 1)]
                mm = None
                for j in range(8):
                    mm = p.on("T", lambda E, oc=oc, j=j, ps_oc=ps_oc, wsb=wsb:
                              E.matmul(ps_oc,
                                       wsb[:, j, 128 * oc:128 * oc + 128],
                                       yfm[:, j, :], start=(j == 0),
                                       stop=(j == 7)),
                              after=(tp_last, tkw, tkb) + p.pw(bank),
                              inc=(1 if j == 7 else 0))
                ev = p.on("S", lambda E, oc=oc, ps_oc=ps_oc, dst=dst, bfm=bfm:
                          E.activation(dst[:, oc, :], ps_oc, AF.Identity,
                                       bias=bfm[:, oc:oc + 1]),
                          after=(mm, fr.get(frk)), inc=1)
                p.pset([bank], ev)
            if frk == "kfm":
                kst = []
                for oc in range(8):
                    dstk = bass.AP(k_out.tensor, (128 * oc) * TPC + g * 512,
                                   [[TPC, 128], [1, 512]])
                    kst.append(p.dma("Q", dstk, kfm[:, oc, :], after=(ev,)))
                spill_toks += kst
                fr["kfm"] = p.barrier_token("Q", tuple(kst))
        vt_toks = []
        free_v = fr.get("vtm")
        for t in range(4):
            for oc in range(8):
                blk = oc % 4
                tt = p.on("T", lambda E, t=t, oc=oc, blk=blk: E.transpose(
                    tp_ps[:, 128 * blk:128 * blk + 128],
                    vfm[:, oc, 128 * t:128 * t + 128], ident[:, :]),
                    after=(ev,) + p.pw(0), inc=(1 if blk == 3 else 0))
                if blk == 3:
                    te = p.on("S", lambda E, t=t, oc=oc: E.activation(
                        vtm[:, t, 128 * (oc - 3):128 * (oc + 1)], tp_ps,
                        AF.Identity), after=(tt, free_v), inc=1)
                    free_v = None
                    p.pset([0], te)
                    vt_toks.append(te)
        fr["vfm"] = vt_toks[-1]
        dstv = bass.AP(v_out.tensor, g * 512 * DM,
                       [[DM, 128], [128 * DM, 4], [1, DM]])
        vst = p.dma("Q", dstv, vtm[:, :, :], after=(vt_toks[-1],))
        spill_toks.append(vst)
        fr["vtm"] = vst

    if ys:
        yss = p.dma("Q", bass.AP(ys_out.tensor, 0, [[32, 128], [1, 32]]),
                    ysum[:, :, :].rearrange("p a b -> p (a b)"),
                    after=(ys_prev,))
        spill_toks.append(yss)
    return p.finalize(spill_toks)


def build_l1_attn_b():

    p = Prog("l1ab")
    NTOK = B * S
    xo_h = p.din("xown", [TPC, DM])
    kf_h = p.din("kfull", [DM, NTOK], BF16)
    vf_h = p.din("vfull", [NTOK, DM], BF16)
    yo_h = p.din("yown", [128, 8, TPC], BF16)
    wq_h = p.din("wq", [128, 8, DM])
    wo_h = p.din("wo", [128, 8, DM])
    bqf_h = p.din("bqf", [128, 8])
    bo_h = p.din("bo", [1, DM])
    mask_h = p.din("mask", [128, 4, S], BF16)
    id_h = p.din("ident", [128, 128], BF16)
    on_h = p.din("onesr", [1, 128], BF16)
    y_h = p.dout("y", [TPC, DM])

    xr = p.sb([128, DM], F32, "xr")
    yfm = p.sb([128, 8, 512], BF16, "yfm")
    qfm = p.sb([128, 8, TPC], BF16, "qfm")
    kbh = p.sb([64, 2, 4096], BF16, "kbh")
    qst = p.sb([64, 512], BF16, "qst")
    vbh = p.sb([128, 32, 128], BF16, "vbh")
    probs = p.sb([128, 2, S], BF16, "probs")
    probsT = p.sb([128, 2, 256], BF16, "probsT")
    dmat = p.sb([128, 4, 128], BF16, "dmat")
    ctxa = p.sb([128, 8, 512], BF16, "ctxa")
    msk = p.sb([128, 4, S], BF16, "msk")
    ncol = p.sb([128, 64], F32, "ncol")
    wab = p.sb([128, 8, DM], BF16, "wab")
    wcd = p.sb([128, 8, DM], BF16, "wcd")
    bqf = p.sb([128, 8], F32, "bqf")
    bkf = p.sb([128, 8], F32, "bkf")
    bvf = p.sb([128, 8], F32, "bvf")
    bo = p.sb([1, DM], BF16, "bo")
    ident = p.sb([128, 128], BF16, "ident")
    onesr = p.sb([1, 128], BF16, "onesr")

    pall = p.psum([128, 4096], F32, "pall")
    tp_ps = pall[:, 0:256].bitcast(BF16)

    tk = {}
    tk["wq"] = p.dma("P", wab[:, :, :], wq_h[:, :, :])
    tk["bo"] = p.dma("P", bo[:, :], bo_h[:, :])
    tk["msk"] = p.dma("P", msk[:, :, :], mask_h[:, :, :])
    for n, sb_, h_ in [("bqf", bqf, bqf_h), ("ident", ident, id_h),
                       ("onesr", onesr, on_h)]:
        tk[n] = p.dma("Q", sb_[:, :], h_[:, :])

    fr = {}
    store_toks = []
    spill_toks = []

    qe = None
    for b in range(B):
        srcy = bass.AP(yo_h.tensor, b * 512,
                       [[8 * TPC, 128], [TPC, 8], [1, 512]])
        tk_y = p.dma("Q", yfm[:, :, :], srcy, after=(fr.get("yfm"),))
        tp_last = tk_y
        for oc in range(8):
            bank = 1 + oc % 2
            ps_oc = pall[:, 512 * bank:512 * (bank + 1)]
            mm = None
            for j in range(8):
                mm = p.on("T", lambda E, oc=oc, j=j, ps_oc=ps_oc: E.matmul(
                    ps_oc, wab[:, j, 128 * oc:128 * oc + 128], yfm[:, j, :],
                    start=(j == 0), stop=(j == 7)),
                    after=(tp_last, tk["wq"], tk["bqf"]) + p.pw(bank),
                    inc=(1 if j == 7 else 0))
            qe = p.on("S", lambda E, oc=oc, b=b, ps_oc=ps_oc: E.activation(
                qfm[:, oc, 512 * b:512 * (b + 1)], ps_oc, AF.Identity,
                bias=bqf[:, oc:oc + 1]), after=(mm,), inc=1)
            p.pset([bank], qe)
        fr["yfm"] = qe
    q_done = qe
    tk["wo"] = p.dma("P", wcd[:, :, :], wo_h[:, :, :])

    kb_free = None
    vb_free = None
    last_tk_v = None
    for b in range(B):
        ctx_evs = []
        for h in range(16):
            hp = h % 2
            srck = bass.AP(kf_h.tensor, (64 * h) * NTOK + b * S,
                           [[NTOK, 64], [1, 4096]])
            tk_k = p.dma("Q", kbh[:, hp, :], srck,
                         after=(fr.get(("kb", hp)),))
            if h % 2 == 0:
                tk_vs = []
                for part in range(4):
                    srcv = bass.AP(
                        vf_h.tensor,
                        (b * S + part * 1024) * DM + 128 * (h // 2),
                        [[DM, 128], [128 * DM, 8], [1, 128]])
                    tk_vs.append(p.dma(
                        "Q", vbh[:, 8 * part:8 * (part + 1), :], srcv,
                        after=(vb_free,)))
                last_tk_v = p.barrier_token("Q", tuple(tk_vs))
            qrow = 64 * (h % 2)
            oc_q = h // 2
            qcp = p.on("V", lambda E, qrow=qrow, oc_q=oc_q, b=b: E.tensor_copy(
                qst[:, :], qfm[qrow:qrow + 64, oc_q, 512 * b:512 * (b + 1)]),
                after=(q_done, fr.get("qst")), inc=1)

            for ph in range(2):
                sm_done = []
                last_mm = None
                for qq in range(2):
                    qt = 2 * ph + qq
                    nb = 16 * qt
                    lhs_q = qst[:, 128 * qt:128 * qt + 128]
                    pmx_toks = []
                    for kc in range(8):
                        last_mm = p.on(
                            "T", lambda E, kc=kc, lhs_q=lhs_q, hp=hp:
                            E.matmul(
                                pall[:, 512 * kc:512 * (kc + 1)], lhs_q,
                                kbh[:, hp, 512 * kc:512 * (kc + 1)],
                                start=True, stop=True),
                            after=(tk_k, qcp) + p.pw(kc), inc=1)
                        mkk = p.on("V", lambda E, kc=kc, qt=qt:
                                   E.scalar_tensor_tensor(
                            pall[:, 512 * kc:512 * (kc + 1)],
                            pall[:, 512 * kc:512 * (kc + 1)], 0.125,
                            msk[:, qt, 512 * kc:512 * (kc + 1)],
                            ALU.mult, ALU.add),
                            after=(last_mm, tk["msk"]), inc=1)
                        pmx = p.on("V", lambda E, kc=kc, nb=nb: E.reduce_max(
                            ncol[:, nb + kc:nb + kc + 1],
                            pall[:, 512 * kc:512 * (kc + 1)], AX.X),
                            after=(mkk,)
                            + ((fr.get(("nc", qt)),) if kc == 0 else ()),
                            inc=1)
                        pmx_toks.append(pmx)
                    ns = p.on("V", lambda E, nb=nb: E.reduce_max(
                        ncol[:, nb + 8:nb + 9], ncol[:, nb:nb + 8], AX.X,
                        negate=True), after=(pmx_toks[-1],), inc=1)
                    ex = p.on("S", lambda E, nb=nb, qq=qq: E.activation(
                        probs[:, qq, :], pall[:, :], AF.Exp, scale=1.0,
                        bias=ncol[:, nb + 8:nb + 9],
                        accum_out=ncol[:, nb + 9:nb + 10]),
                        after=(ns, fr.get("probs")), inc=1)
                    p.pset(list(range(8)), ex)
                    rd = p.on("V", lambda E, nb=nb: E.reciprocal(
                        ncol[:, nb + 10:nb + 11],
                        ncol[:, nb + 9:nb + 10]), after=(ex,), inc=1)
                    dm = p.on("V", lambda E, qt=qt, nb=nb: E.tensor_scalar(
                        dmat[:, qt, :], ident[:, :],
                        ncol[:, nb + 10:nb + 11], None, ALU.mult),
                        after=(rd, tk["ident"], fr.get("dmat")), inc=1)
                    fr[("nc", qt)] = dm
                    sm_done.append((ex, dm))
                fr[("kb", hp)] = last_mm

                pt_bk = (2, 3)
                ctx_bank = 4 + ph
                ctx_mm = None
                t_mms = None
                ev_list = [None] * 32

                def emit_pT(kappa):
                    pb = pt_bk[kappa % 2]
                    kp = kappa % 2
                    tm = []
                    for qq in range(2):
                        qt = 2 * ph + qq
                        tm.append(p.on(
                            "T", lambda E, kappa=kappa, qq=qq, qt=qt, pb=pb:
                            E.matmul(
                                pall[:, 512 * pb + qq * 128:
                                     512 * pb + qq * 128 + 128],
                                probs[:, qq, 128 * kappa:128 * kappa + 128],
                                dmat[:, qt, :], start=True, stop=True),
                            after=(sm_done[qq][0], sm_done[qq][1]) + p.pw(pb),
                            inc=(1 if qq == 1 else 0)))
                    ev_list[kappa] = p.on(
                        "S", lambda E, kp=kp, pb=pb: E.activation(
                            probsT[:, kp, :],
                            pall[:, 512 * pb:512 * pb + 256], AF.Identity),
                        after=(tm[-1], fr.get(("pT", kp))), inc=1)
                    p.pset([pb], ev_list[kappa])
                    return tm

                def emit_ctx(kappa):
                    kp = kappa % 2
                    cm = p.on("T", lambda E, kappa=kappa, qrow=qrow,
                              kp=kp, ctx_bank=ctx_bank:
                              E.matmul(pall[0:64, 512 * ctx_bank:
                                           512 * ctx_bank + 256],
                                       vbh[:, kappa, qrow:qrow + 64],
                                       probsT[:, kp, :],
                                       start=(kappa == 0),
                                       stop=(kappa == 31)),
                              after=(ev_list[kappa], last_tk_v)
                              + p.pw(ctx_bank),
                              inc=(1 if kappa == 31 else 0))
                    fr[("pT", kp)] = cm
                    return cm

                for kappa in range(33):
                    if kappa < 32:
                        t_mms = emit_pT(kappa)
                    if kappa >= 1:
                        ctx_mm = emit_ctx(kappa - 1)
                ce = p.on("S", lambda E, qrow=qrow, oc_q=oc_q, ph=ph,
                          ctx_bank=ctx_bank: E.activation(
                    ctxa[qrow:qrow + 64, oc_q, 256 * ph:256 * (ph + 1)],
                    pall[0:64, 512 * ctx_bank:512 * ctx_bank + 256],
                    AF.Identity),
                    after=(ctx_mm, fr.get("ctxa")), inc=1)
                p.pset([ctx_bank], ce)
                ctx_evs.append(ce)
                fr["probs"] = t_mms[-1]
                fr["dmat"] = t_mms[-1]
            if h % 2 == 1:
                vb_free = ctx_mm
            fr["qst"] = last_mm

        omm_last = None
        for qt in range(4):
            t = b * 4 + qt
            srcx = bass.AP(xo_h.tensor, t * 128 * DM, [[DM, 128], [1, DM]])
            tk_xr = p.dma("Q", xr[:, :], srcx, after=(fr.get("xr"),))
            ps_o = pall[:, 1024:2048]
            mm = None
            for sl in range(2):
                p.on("T", lambda E, sl=sl: E.matmul(
                    ps_o[:, sl * 512:(sl + 1) * 512], onesr[:, :],
                    bo[:, sl * 512:(sl + 1) * 512], start=True, stop=False),
                    after=tuple(ctx_evs) + (tk["onesr"], tk["bo"], tk["wo"])
                    + p.pw(2, 3), inc=0)
                for j in range(8):
                    mm = p.on("T", lambda E, sl=sl, j=j, qt=qt: E.matmul(
                        ps_o[:, sl * 512:(sl + 1) * 512],
                        ctxa[:, j, 128 * qt:128 * qt + 128],
                        wcd[:, j, sl * 512:(sl + 1) * 512],
                        start=False, stop=(j == 7)),
                        after=(), inc=(1 if j == 7 else 0))
            orr = p.on("V", lambda E: E.tensor_tensor(
                xr[:, :], ps_o, xr[:, :], ALU.add), after=(mm, tk_xr), inc=1)
            p.pset([2, 3], orr)
            dst = bass.AP(y_h.tensor, t * 128 * DM, [[DM, 128], [1, DM]])
            st = p.dma("Q", dst, xr[:, :], after=(orr,))
            store_toks.append(st)
            fr["xr"] = st
            omm_last = mm
        fr["ctxa"] = omm_last

    return p.finalize(store_toks)


def build_l0_attn_a():
    p = Prog("l0aa")
    NTOK = B * S
    xo_h = p.din("xown", [TPC, DM])
    kf_h = p.din("kfull", [DM, NTOK], BF16)
    vf_h = p.din("vfull", [NTOK, DM], BF16)
    yo_h = p.din("yown", [128, 8, TPC], BF16)
    ysm_h = p.din("ysums", [128, 32, 8])
    wq_h = p.din("wq", [128, 8, DM])
    wo_h = p.din("wo", [128, 8, DM])
    wfq_h = p.din("wfq", [128, 8, DM])
    wfg_h = p.din("wfg", [128, 8, DM])
    bqf_h = p.din("bqf", [128, 8])
    bfqf_h = p.din("bfqf", [128, 8])
    bfgf_h = p.din("bfgf", [128, 8])
    bo_h = p.din("bo", [1, DM])
    updiag_h = p.din("updiag", [8, 128, 48], BF16)
    sel2_h = p.din("sel2", [128, 32], BF16)
    iota_h = p.din("iota2", [2, 4096], BF16)
    id_h = p.din("ident", [128, 128], BF16)
    idf_h = p.din("identf", [128, 16])
    on_h = p.din("onesr", [1, 128], BF16)
    y_h = p.dout("y", [TPC, DM])

    xr = p.sb([128, DM], F32, "xr")
    yfm = p.sb([128, 8, 512], BF16, "yfm")
    qfm = p.sb([128, 8, TPC], BF16, "qfm")
    kbh = p.sb([66, 2, 4096], BF16, "kbh")
    qst = p.sb([66, 512], BF16, "qst")
    vbh = p.sb([128, 32, 128], BF16, "vbh")
    probs = p.sb([128, 2, S], BF16, "probs")
    probsT = p.sb([128, 2, 256], BF16, "probsT")
    dmat = p.sb([128, 4, 128], BF16, "dmat")
    ctxa = p.sb([128, 8, 512], BF16, "ctxa")
    cfm = p.sb([128, 8, 512], BF16, "cfm")
    s2all = p.sb([128, 4, 512], BF16, "s2all")
    s1 = p.sb([16, 2048], F32, "s1")
    cprime = p.sb([128, 4, 4, 16], F32, "cprime")
    ysum = p.sb([128, 8, 4], F32, "ysum")
    ybarb = p.sb([128, 8, 4], BF16, "ybarb")
    glob = p.sb([128, 8, 4], BF16, "glob")
    ccol = p.sb([128, 8, 4], F32, "ccol")
    ys8 = p.sb([128, 32, 8], F32, "ys8")
    ncol = p.sb([128, 64], F32, "ncol")
    wab = p.sb([128, 8, DM], BF16, "wab")
    wcd = p.sb([128, 8, DM], BF16, "wcd")
    bqf = p.sb([128, 8], F32, "bqf")
    bkf = p.sb([128, 8], F32, "bkf")
    bvf = p.sb([128, 8], F32, "bvf")
    bfqf = p.sb([128, 8], F32, "bfqf")
    bfgf = p.sb([128, 8], F32, "bfgf")
    bo = p.sb([1, DM], BF16, "bo")
    updiag = p.sb([128, 8, 48], BF16, "updiag")
    sel2 = p.sb([128, 32], BF16, "sel2")
    iota2 = p.sb([2, 4096], BF16, "iota2")
    ident = p.sb([128, 128], BF16, "ident")
    identf = p.sb([128, 16], F32, "identf")
    onesr = p.sb([1, 128], BF16, "onesr")

    pall = p.psum([128, 4096], F32, "pall")
    tp_ps = pall[:, 0:256].bitcast(BF16)

    tk = {}
    tk["wq"] = p.dma("P", wab[:, :, :], wq_h[:, :, :])
    tk["bo"] = p.dma("P", bo[:, :], bo_h[:, :])
    tk["ys"] = p.dma("P", ys8[:, :, :], ysm_h[:, :, :])
    tk["updiag"] = p.dma("Q", updiag[:, :, :],
                         bass.AP(updiag_h.tensor, 0,
                                 [[48, 128], [128 * 48, 8], [1, 48]]))
    for n, sb_, h_ in [("bqf", bqf, bqf_h), ("bfqf", bfqf, bfqf_h),
                       ("bfgf", bfgf, bfgf_h), ("ident", ident, id_h),
                       ("identf", identf, idf_h), ("onesr", onesr, on_h),
                       ("sel2", sel2, sel2_h), ("iota2", iota2, iota_h)]:
        tk[n] = p.dma("Q", sb_[(slice(None),) * 2], h_[(slice(None),) * 2])

    fr = {}
    store_toks = []
    spill_toks = []
    ys_toks = []

    s2init = p.on("V", lambda E: E.memset(s2all[:, :, :], 0.0), after=(),
                  inc=1)
    ys_done = p.on("V", lambda E: E.tensor_reduce(
        ysum[:, :, :].rearrange("p a b -> p (a b)").unsqueeze(2),
        ys8[:, :, :], AX.X, ALU.add), after=(tk["ys"],), inc=1)

    qe = None
    for b in range(B):
        srcy = bass.AP(yo_h.tensor, b * 512,
                       [[8 * TPC, 128], [TPC, 8], [1, 512]])
        tk_y = p.dma("Q", yfm[:, :, :], srcy, after=(fr.get("yfm"),))
        tp_last = tk_y
        for oc in range(8):
            bank = 1 + oc % 2
            ps_oc = pall[:, 512 * bank:512 * (bank + 1)]
            mm = None
            for j in range(8):
                mm = p.on("T", lambda E, oc=oc, j=j, ps_oc=ps_oc: E.matmul(
                    ps_oc, wab[:, j, 128 * oc:128 * oc + 128], yfm[:, j, :],
                    start=(j == 0), stop=(j == 7)),
                    after=(tp_last, tk["wq"], tk["bqf"]) + p.pw(bank),
                    inc=(1 if j == 7 else 0))
            qe = p.on("S", lambda E, oc=oc, b=b, ps_oc=ps_oc: E.activation(
                qfm[:, oc, 512 * b:512 * (b + 1)], ps_oc, AF.Identity,
                bias=bqf[:, oc:oc + 1]), after=(mm,), inc=1)
            p.pset([bank], qe)
        fr["yfm"] = qe
    q_done = qe

    yb = p.on("V", lambda E: E.tensor_scalar_mul(
        ybarb[:, :, :], ysum[:, :, :], 1.0 / S), after=(ys_done,), inc=1)
    glo_evs = []
    ge = None
    for oc in range(8):
        mm = None
        for j in range(8):
            mm = p.on("T", lambda E, oc=oc, j=j: E.matmul(
                pall[:, 3072 + 4 * oc:3072 + 4 * oc + 4],
                wab[:, j, 128 * oc:128 * oc + 128], ybarb[:, j, :],
                start=(j == 0), stop=(j == 7)),
                after=(yb, tk["wq"], q_done) + p.pw(6),
                inc=(1 if j == 7 else 0))
        ge = p.on("S", lambda E, oc=oc: E.activation(
            glob[:, oc, :], pall[:, 3072 + 4 * oc:3072 + 4 * oc + 4],
            AF.Identity, bias=bqf[:, oc:oc + 1]), after=(mm,), inc=1)
        p.pset([6], ge)
        glo_evs.append(ge)
    tk["wfg"] = p.dma("P", wab[:, :, :], wfg_h[:, :, :], after=(ge,))
    gw_evs = []
    for oc in range(8):
        mm = None
        for j in range(8):
            mm = p.on("T", lambda E, oc=oc, j=j: E.matmul(
                pall[:, 3584 + 4 * oc:3584 + 4 * oc + 4],
                wab[:, j, 128 * oc:128 * oc + 128], glob[:, j, :],
                start=(j == 0), stop=(j == 7)),
                after=(tk["wfg"], glo_evs[-1], tk["bfgf"]) + p.pw(7),
                inc=(1 if j == 7 else 0))
        ge2 = p.on("S", lambda E, oc=oc: E.activation(
            ccol[:, oc, :], pall[:, 3584 + 4 * oc:3584 + 4 * oc + 4],
            AF.Identity, bias=bfgf[:, oc:oc + 1]), after=(mm,), inc=1)
        p.pset([7], ge2)
        gw_evs.append(ge2)
    cc = p.on("V", lambda E: E.tensor_tensor(
        ccol[:, :, :], ccol[:, :, :],
        bfqf[:, :].unsqueeze(2).to_broadcast([128, 8, 4]), ALU.add),
        after=(gw_evs[-1], tk["bfqf"]), inc=1)

    tk["wfq"] = p.dma("P", wab[:, :, :], wfq_h[:, :, :], after=(gw_evs[-1],))
    sel_free = None
    for b in range(B):
        ce_toks = []
        for oc in range(8):
            bank = 1 + oc % 2
            ps_oc = pall[:, 512 * bank:512 * (bank + 1)]
            mm = None
            for j in range(8):
                mm = p.on("T", lambda E, oc=oc, j=j, b=b, ps_oc=ps_oc:
                          E.matmul(ps_oc,
                                   wab[:, j, 128 * oc:128 * oc + 128],
                                   qfm[:, j, 512 * b:512 * (b + 1)],
                                   start=(j == 0), stop=(j == 7)),
                          after=(tk["wfq"], q_done, cc) + p.pw(bank),
                          inc=(1 if j == 7 else 0))
            cev = p.on("S", lambda E, oc=oc, b=b, ps_oc=ps_oc: E.activation(
                cfm[:, oc, :], ps_oc, AF.Tanh, bias=ccol[:, oc, b:b + 1]),
                after=(mm, fr.get("cfm")), inc=1)
            p.pset([bank], cev)
            ce_toks.append(cev)
        mm = None
        for j in range(8):
            mm = p.on("T", lambda E, j=j: E.matmul(
                pall[0:48, 2560:3072], updiag[:, j, :], cfm[:, j, :],
                start=(j == 0), stop=(j == 7)),
                after=(ce_toks[-1], tk["updiag"]) + p.pw(5),
                inc=(1 if j == 7 else 0))
        fr["cfm"] = mm
        a1 = p.on("S", lambda E: E.activation(
            s1[0:16, 0:512], pall[0:16, 2560:3072], AF.Exp, scale=-1.0),
            after=(mm, fr.get("s1")), inc=1)
        a1b = p.on("S", lambda E: E.activation(
            s1[0:16, 512:1024], pall[32:48, 2560:3072], AF.Exp, scale=-1.0),
            after=(mm, fr.get("s1")), inc=1)
        p.pset([5], a1b)
        a2 = p.on("V", lambda E: E.tensor_scalar_add(
            s1[0:16, 0:1024], s1[0:16, 0:1024], 1.0), after=(a1, a1b), inc=1)
        a3 = p.on("V", lambda E: E.reciprocal(
            s1[0:16, 0:1024], s1[0:16, 0:1024]), after=(a2,), inc=1)
        a4 = p.on("V", lambda E: E.tensor_scalar(
            s1[0:16, 0:512], s1[0:16, 0:512], 4096.0, -2048.0, ALU.mult,
            ALU.add), after=(a3,), inc=1)
        a5 = p.on("V", lambda E: E.tensor_scalar_mul(
            s1[0:16, 512:1024], s1[0:16, 512:1024], 4096.0),
            after=(a3,), inc=1)
        a6 = p.on("S", lambda E: E.activation(
            s1[0:16, 512:1024], s1[0:16, 512:1024], AF.Square),
            after=(a5,), inc=1)
        a7 = p.on("V", lambda E: E.reciprocal(
            s1[0:16, 512:1024], s1[0:16, 512:1024]), after=(a6,), inc=1)
        a8_ = p.on("V", lambda E: E.tensor_scalar_mul(
            s1[0:16, 512:1024], s1[0:16, 512:1024], -16.0), after=(a7,), inc=1)
        a9 = p.on("V", lambda E, b=b: E.tensor_copy(
            s2all[32:48, b, :], s1[0:16, 512:1024]),
            after=(a8_, s2init, fr.get("s2a")), inc=1)
        a10 = p.on("V", lambda E: E.tensor_tensor(
            s1[0:16, 1024:1536], s1[0:16, 512:1024], s1[0:16, 0:512],
            ALU.mult), after=(a9, a4), inc=1)
        a11 = p.on("V", lambda E, b=b: E.tensor_scalar_mul(
            s2all[64:80, b, :], s1[0:16, 1024:1536], -2.0),
            after=(a10,), inc=1)
        a12 = p.on("V", lambda E: E.tensor_tensor(
            s1[0:16, 1536:2048], s1[0:16, 1024:1536], s1[0:16, 0:512],
            ALU.mult), after=(a11,), inc=1)
        ctp = None
        for qt in range(4):
            tt = p.on("T", lambda E, qt=qt: E.transpose(
                pall[:, 3072:3088],
                s1[0:16, 1536 + 128 * qt:1536 + 128 * qt + 128],
                identf[0:16, :]),
                after=(a12, tk["identf"]) + p.pw(6), inc=1)
            ctp = p.on("S", lambda E, b=b, qt=qt: E.activation(
                cprime[:, b, qt, :], pall[:, 3072:3088], AF.Identity),
                after=(tt, fr.get("cpr")), inc=1)
            p.pset([6], ctp)
        fr["s1"] = ctp
        fr["s2a"] = None
        fr["cpr"] = None
    s2_done = a11
    cp_done = ctp

    tk["wo"] = p.dma("P", wcd[:, :, :], wo_h[:, :, :])

    kb_free = None
    vb_free = None
    last_tk_v = None
    for b in range(B):
        ctx_evs = []
        for h in range(16):
            it = b * 16 + h
            hp = h % 2
            srck = bass.AP(kf_h.tensor, (64 * h) * NTOK + b * S,
                           [[NTOK, 64], [1, 4096]])
            tk_k = p.dma("Q", kbh[0:64, hp, :], srck,
                         after=(fr.get(("kb", hp)),))
            kaug = p.on("V", lambda E, hp=hp: E.tensor_copy(
                kbh[64:66, hp, :], iota2[:, :]),
                after=(tk_k, tk["iota2"], fr.get(("kb", hp))), inc=1)
            if h % 2 == 0:
                tk_vs = []
                for part in range(4):
                    srcv = bass.AP(
                        vf_h.tensor,
                        (b * S + part * 1024) * DM + 128 * (h // 2),
                        [[DM, 128], [128 * DM, 8], [1, 128]])
                    tk_vs.append(p.dma(
                        "Q", vbh[:, 8 * part:8 * (part + 1), :], srcv,
                        after=(vb_free,)))
                last_tk_v = p.barrier_token("Q", tuple(tk_vs))
            qrow = 64 * (h % 2)
            oc_q = h // 2
            qcp = p.on("V", lambda E, qrow=qrow, oc_q=oc_q, b=b: E.tensor_copy(
                qst[0:64, :], qfm[qrow:qrow + 64, oc_q, 512 * b:512 * (b + 1)]),
                after=(q_done, fr.get("qst")), inc=1)
            smm = p.on("T", lambda E, h=h, b=b: E.matmul(
                pall[0:2, 3072:3584], sel2[:, 2 * h:2 * h + 2],
                s2all[:, b, :], start=True, stop=True),
                after=(s2_done, tk["sel2"]) + p.pw(6), inc=1)
            sev = p.on("V", lambda E: E.tensor_copy(qst[64:66, :],
                                                    pall[0:2, 3072:3584]),
                       after=(smm, qcp), inc=1)
            p.pset([6], sev)

            for ph in range(2):
                sm_done = []
                last_mm = None
                for qq in range(2):
                    qt = 2 * ph + qq
                    nb = 16 * qt
                    lhs_q = qst[:, 128 * qt:128 * qt + 128]
                    pmx_toks = []
                    for kc in range(8):
                        last_mm = p.on(
                            "T", lambda E, kc=kc, lhs_q=lhs_q, hp=hp:
                            E.matmul(
                                pall[:, 512 * kc:512 * (kc + 1)], lhs_q,
                                kbh[:, hp, 512 * kc:512 * (kc + 1)],
                                start=True, stop=True),
                            after=(kaug, sev) + p.pw(kc), inc=1)
                        pmx = p.on("V", lambda E, kc=kc, nb=nb: E.reduce_max(
                            ncol[:, nb + kc:nb + kc + 1],
                            pall[:, 512 * kc:512 * (kc + 1)], AX.X),
                            after=(last_mm,)
                            + ((fr.get(("nc", qt)),) if kc == 0 else ()),
                            inc=1)
                        pmx_toks.append(pmx)
                    nm = p.on("V", lambda E, nb=nb: E.reduce_max(
                        ncol[:, nb + 8:nb + 9], ncol[:, nb:nb + 8], AX.X,
                        negate=True), after=(pmx_toks[-1],), inc=1)
                    ns1 = p.on("V", lambda E, nb=nb: E.tensor_scalar_mul(
                        ncol[:, nb + 8:nb + 9], ncol[:, nb + 8:nb + 9],
                        0.125), after=(nm,), inc=1)
                    ns = p.on("V", lambda E, nb=nb, b=b, h=h, qt=qt:
                              E.scalar_tensor_tensor(
                        ncol[:, nb + 8:nb + 9],
                        cprime[:, b, qt, h:h + 1], 0.125,
                        ncol[:, nb + 8:nb + 9], ALU.mult, ALU.add),
                        after=(ns1, cp_done), inc=1)
                    ex = p.on("S", lambda E, nb=nb, qq=qq: E.activation(
                        probs[:, qq, :], pall[:, :], AF.Exp, scale=0.125,
                        bias=ncol[:, nb + 8:nb + 9],
                        accum_out=ncol[:, nb + 9:nb + 10]),
                        after=(ns, fr.get("probs")), inc=1)
                    p.pset(list(range(8)), ex)
                    rd = p.on("V", lambda E, nb=nb: E.reciprocal(
                        ncol[:, nb + 10:nb + 11],
                        ncol[:, nb + 9:nb + 10]), after=(ex,), inc=1)
                    dm = p.on("V", lambda E, qt=qt, nb=nb: E.tensor_scalar(
                        dmat[:, qt, :], ident[:, :],
                        ncol[:, nb + 10:nb + 11], None, ALU.mult),
                        after=(rd, tk["ident"], fr.get("dmat")), inc=1)
                    fr[("nc", qt)] = dm
                    sm_done.append((ex, dm))
                fr[("kb", hp)] = last_mm

                pt_bk = (2, 3)
                ctx_bank = 4 + ph
                ctx_mm = None
                t_mms = None
                ev_list = [None] * 32

                def emit_pT(kappa, ph=ph, sm_done=sm_done):
                    pb = pt_bk[kappa % 2]
                    kp = kappa % 2
                    tm = []
                    for qq in range(2):
                        qt = 2 * ph + qq
                        tm.append(p.on(
                            "T", lambda E, kappa=kappa, qq=qq, qt=qt, pb=pb:
                            E.matmul(
                                pall[:, 512 * pb + qq * 128:
                                     512 * pb + qq * 128 + 128],
                                probs[:, qq, 128 * kappa:128 * kappa + 128],
                                dmat[:, qt, :], start=True, stop=True),
                            after=(sm_done[qq][0], sm_done[qq][1]) + p.pw(pb),
                            inc=(1 if qq == 1 else 0)))
                    ev_list[kappa] = p.on(
                        "S", lambda E, kp=kp, pb=pb: E.activation(
                            probsT[:, kp, :],
                            pall[:, 512 * pb:512 * pb + 256], AF.Identity),
                        after=(tm[-1], fr.get(("pT", kp))), inc=1)
                    p.pset([pb], ev_list[kappa])
                    return tm

                def emit_ctx(kappa, qrow=qrow, ctx_bank=ctx_bank):
                    kp = kappa % 2
                    cm = p.on("T", lambda E, kappa=kappa, qrow=qrow,
                              kp=kp, ctx_bank=ctx_bank:
                              E.matmul(pall[0:64, 512 * ctx_bank:
                                           512 * ctx_bank + 256],
                                       vbh[:, kappa, qrow:qrow + 64],
                                       probsT[:, kp, :],
                                       start=(kappa == 0),
                                       stop=(kappa == 31)),
                              after=(ev_list[kappa], last_tk_v)
                              + p.pw(ctx_bank),
                              inc=(1 if kappa == 31 else 0))
                    fr[("pT", kp)] = cm
                    return cm

                for kappa in range(33):
                    if kappa < 32:
                        t_mms = emit_pT(kappa)
                    if kappa >= 1:
                        ctx_mm = emit_ctx(kappa - 1)
                ce = p.on("S", lambda E, qrow=qrow, oc_q=oc_q, ph=ph,
                          ctx_bank=ctx_bank: E.activation(
                    ctxa[qrow:qrow + 64, oc_q, 256 * ph:256 * (ph + 1)],
                    pall[0:64, 512 * ctx_bank:512 * ctx_bank + 256],
                    AF.Identity),
                    after=(ctx_mm, fr.get("ctxa")), inc=1)
                p.pset([ctx_bank], ce)
                ctx_evs.append(ce)
                fr["probs"] = t_mms[-1]
                fr["dmat"] = t_mms[-1]
            if h % 2 == 1:
                vb_free = ctx_mm
            fr["qst"] = last_mm

        omm_last = None
        for qt in range(4):
            t = b * 4 + qt
            srcx = bass.AP(xo_h.tensor, t * 128 * DM, [[DM, 128], [1, DM]])
            tk_xr = p.dma("Q", xr[:, :], srcx, after=(fr.get("xr"),))
            ps_o = pall[:, 1024:2048]
            mm = None
            for sl in range(2):
                p.on("T", lambda E, sl=sl: E.matmul(
                    ps_o[:, sl * 512:(sl + 1) * 512], onesr[:, :],
                    bo[:, sl * 512:(sl + 1) * 512], start=True, stop=False),
                    after=tuple(ctx_evs) + (tk["onesr"], tk["bo"], tk["wo"])
                    + p.pw(2, 3), inc=0)
                for j in range(8):
                    mm = p.on("T", lambda E, sl=sl, j=j, qt=qt: E.matmul(
                        ps_o[:, sl * 512:(sl + 1) * 512],
                        ctxa[:, j, 128 * qt:128 * qt + 128],
                        wcd[:, j, sl * 512:(sl + 1) * 512],
                        start=False, stop=(j == 7)),
                        after=(), inc=(1 if j == 7 else 0))
            orr = p.on("V", lambda E: E.tensor_tensor(
                xr[:, :], ps_o, xr[:, :], ALU.add), after=(mm, tk_xr), inc=1)
            p.pset([2, 3], orr)
            dst = bass.AP(y_h.tensor, t * 128 * DM, [[DM, 128], [1, DM]])
            st = p.dma("Q", dst, xr[:, :], after=(orr,))
            store_toks.append(st)
            fr["xr"] = st
            omm_last = mm
        fr["ctxa"] = omm_last

    return p.finalize(store_toks)


def _causal_mask(core):
    k = np.arange(S)[None, None, :]
    thr = 512 * core + 128 * np.arange(4)[None, :, None] + \
        np.arange(128)[:, None, None]
    return np.where(k <= thr, 0.0, -1e30).astype(bf16)


def _l1abkv_inputs(core, x_own, w_in, b_in):
    w_in = np.asarray(w_in); b_in = np.asarray(b_in)
    r = lambda w: np.ascontiguousarray(
        np.asarray(w).reshape(8, 128, DM).transpose(1, 0, 2))
    bf_ = lambda v: np.ascontiguousarray(
        np.asarray(v).reshape(8, 128).T.astype(np.float32))
    return {
        "xown": np.ascontiguousarray(x_own, dtype=np.float32),
        "wk": r(w_in[DM:2 * DM].T), "wv": r(w_in[2 * DM:].T),
        "bkf": bf_(b_in[DM:2 * DM]), "bvf": bf_(b_in[2 * DM:]),
        "ident": np.eye(128, dtype=np.float32).astype(bf16),
    }


def _l1attnb_inputs(core, x_own, kfull, vfull, yown, w_in, b_in, w_out,
                    b_out):
    w_in = np.asarray(w_in); b_in = np.asarray(b_in)
    r = lambda w: np.ascontiguousarray(
        np.asarray(w).reshape(8, 128, DM).transpose(1, 0, 2))
    bf_ = lambda v: np.ascontiguousarray(
        np.asarray(v).reshape(8, 128).T.astype(np.float32))
    return {
        "xown": np.ascontiguousarray(x_own, dtype=np.float32),
        "kfull": kfull, "vfull": vfull, "yown": yown,
        "wq": r(w_in[0:DM].T), "wo": r(np.asarray(w_out).T),
        "bqf": bf_(b_in[0:DM]),
        "bo": np.asarray(b_out).reshape(1, DM).astype(np.float32),
        "mask": _causal_mask(core),
        "ident": np.eye(128, dtype=np.float32).astype(bf16),
        "onesr": np.ones((1, 128), dtype=np.float32).astype(bf16),
    }


def _gather_kv(kshards, vshards):
    NTOK = B * S
    kfull = np.empty((DM, NTOK), dtype=bf16)
    vfull = np.empty((NTOK, DM), dtype=bf16)
    for c in range(NCORE):
        ks = np.asarray(kshards[c])
        vs = np.asarray(vshards[c])
        for b_ in range(B):
            kfull[:, b_ * S + 512 * c:b_ * S + 512 * (c + 1)] = \
                ks[:, b_ * 512:(b_ + 1) * 512]
            vfull[b_ * S + 512 * c:b_ * S + 512 * (c + 1), :] = \
                vs[b_ * 512:(b_ + 1) * 512, :]
    return kfull, vfull


def run_l1_attn_b(x_full, w_in, b_in, w_out, b_out, cores=None):
    shards = _shard_tokens(x_full)
    cores = cores if cores is not None else list(range(NCORE))
    nckv = build_l1ab_kv()
    maps_kv = [_l1abkv_inputs(c, shards[c], w_in, b_in) for c in cores]
    res_kv = run_bass_kernel_spmd(nckv, maps_kv, core_ids=cores)
    kfull, vfull = _gather_kv([r["kout"] for r in res_kv.results],
                              [r["vout"] for r in res_kv.results])
    youts = [np.asarray(r["yout"]) for r in res_kv.results]
    nc = build_l1_attn_b()
    maps = [_l1attnb_inputs(c, shards[c], kfull, vfull, youts[ci], w_in,
                            b_in, w_out, b_out)
            for ci, c in enumerate(cores)]
    res = run_bass_kernel_spmd(nc, maps, core_ids=cores)
    if len(cores) == NCORE:
        return _unshard_tokens([r["y"] for r in res.results])
    return [np.asarray(r["y"]) for r in res.results]


def _l0attna_inputs(core, x_own, kfull, vfull, yown, ysums, wq, bq_, wk, bk_,
                    wv, bv_, wo, bo_, wfq, bfq_, wfg, bfg_, up, uz):
    r = lambda w: np.ascontiguousarray(
        np.asarray(w).reshape(8, 128, DM).transpose(1, 0, 2))
    bf_ = lambda v: np.ascontiguousarray(
        np.asarray(v).reshape(8, 128).T.astype(np.float32))
    up = np.asarray(up).reshape(16, 64)
    uz = np.asarray(uz).reshape(16, 64)
    updiag = np.zeros((8, 128, 48), dtype=np.float32)
    for h in range(16):
        j = h // 2
        r0 = 64 * (h % 2)
        updiag[j, r0:r0 + 64, h] = up[h]
        updiag[j, r0:r0 + 64, 32 + h] = uz[h]
    sel2 = np.zeros((128, 32), dtype=np.float32)
    for h in range(16):
        sel2[64 + h, 2 * h] = 1.0
        sel2[32 + h, 2 * h + 1] = 1.0
    jj = np.arange(S, dtype=np.float64) - 2048.0
    iota2 = np.stack([jj, jj * jj]).astype(np.float32)
    identf = np.zeros((128, 16), dtype=np.float32)
    identf[0:16, :] = np.eye(16)
    return {
        "xown": np.ascontiguousarray(x_own, dtype=np.float32),
        "kfull": kfull, "vfull": vfull, "yown": yown, "ysums": ysums,
        "wq": r(wq), "wo": r(wo),
        "wfq": r(wfq), "wfg": r(wfg),
        "bqf": bf_(bq_),
        "bfqf": bf_(bfq_), "bfgf": bf_(bfg_),
        "bo": np.asarray(bo_).reshape(1, DM).astype(np.float32),
        "updiag": updiag.astype(bf16), "sel2": sel2.astype(bf16),
        "iota2": iota2.astype(bf16), "identf": identf,
        "ident": np.eye(128, dtype=np.float32).astype(bf16),
        "onesr": np.ones((1, 128), dtype=np.float32).astype(bf16),
    }


def _l0aakv_inputs(core, x_own, wk, bk_, wv, bv_):
    r = lambda w: np.ascontiguousarray(
        np.asarray(w).reshape(8, 128, DM).transpose(1, 0, 2))
    bf_ = lambda v: np.ascontiguousarray(
        np.asarray(v).reshape(8, 128).T.astype(np.float32))
    return {
        "xown": np.ascontiguousarray(x_own, dtype=np.float32),
        "wk": r(wk), "wv": r(wv),
        "bkf": bf_(bk_), "bvf": bf_(bv_),
        "ident": np.eye(128, dtype=np.float32).astype(bf16),
    }


def run_l0_attn_a(x_full, *args, cores=None):
    shards = _shard_tokens(x_full)
    cores = cores if cores is not None else list(range(NCORE))
    nckv = build_l1ab_kv(ys=True)
    maps_kv = [_l0aakv_inputs(c, shards[c], args[2], args[3], args[4],
                              args[5]) for c in cores]
    res_kv = run_bass_kernel_spmd(nckv, maps_kv, core_ids=cores)
    kfull, vfull = _gather_kv([r["kout"] for r in res_kv.results],
                              [r["vout"] for r in res_kv.results])
    ysums = np.ascontiguousarray(np.stack(
        [np.asarray(r["ysout"]) for r in res_kv.results],
        axis=-1).astype(np.float32))
    nc = build_l0_attn_a()
    maps = [_l0attna_inputs(c, shards[c], kfull, vfull,
                            np.asarray(res_kv.results[ci]["yout"]), ysums,
                            *args) for ci, c in enumerate(cores)]
    res = run_bass_kernel_spmd(nc, maps, core_ids=cores)
    if len(cores) == NCORE:
        return _unshard_tokens([r["y"] for r in res.results])
    return [np.asarray(r["y"]) for r in res.results]



def build_l1f_pre():
    p = Prog("l1fpre")
    x_h = p.din("x", [TPC, DM])
    y_h = p.dout("y", [TPC, DM])
    xg = p.sb([128, 2, DM], F32, "xg")
    yt = p.sb([128, 2, DM], F32, "yt")
    cent = p.sb([128, DM], F32, "cent")
    sq = p.sb([128, DM], F32, "sq")
    col4 = p.sb([128, 8], F32, "col4")
    sc = {"cent": cent, "sq": sq, "col4": col4}
    store_toks = []
    fr = {}
    for t in range(16):
        bi = t % 2
        src = bass.AP(x_h.tensor, t * 128 * DM, [[DM, 128], [1, DM]])
        tk_x = p.dma("Q", xg[:, bi, :], src, after=(fr.get(f"x{bi}"),))
        ty, _ = emit_ln_tile(p, xg[:, bi, :], yt[:, bi, :], sc, t % 2,
                             after=(tk_x, fr.get(f"y{bi}")),
                             prev_cent_reader=fr.get("cent"))
        fr["cent"] = ty
        dst = bass.AP(y_h.tensor, t * 128 * DM, [[DM, 128], [1, DM]])
        st = p.dma("Q", dst, yt[:, bi, :], after=(ty,))
        store_toks.append(st)
        fr[f"x{bi}"] = ty
        fr[f"y{bi}"] = st
    return p.finalize(store_toks)


def build_l1f_ffn(final=False):
    p = Prog("l1fffn" + ("F" if final else ""))
    zin_h = p.din("zin", [B, 640, DM])
    w1_h = p.din("w1", [2, 22, 128, 8, 128])
    b1a_h = p.din("b1a", [128, 22])
    b1g_h = p.din("b1g", [128, 22])
    w2_h = p.din("w2", [22, 2, 128, 512])
    b2_h = p.din("b2", [1, DM])
    g_h = p.din("lng", [128, 8])
    bl_h = p.din("lnb", [128, 8])
    id_h = p.din("ident", [128, 128], BF16)
    on_h = p.din("onesr", [1, 128], BF16)
    onf_h = p.din("onesf", [1, 128])
    ob_h = p.din("onesb", [128, 1], BF16)
    if final:
        xa_h = p.din("xadd", [TPC, DM])
    y_h = p.dout("y", [TPC, DM])

    zt = p.sb([128, 5, DM], F32, "zt")
    xn = p.sb([128, 4, 8, 512], BF16, "xn")
    hga = p.sb([128, 4, 22, 512], BF16, "hga")
    zbf = hga[:, 0, 0:10, :].rearrange("p a b -> p (a b)").rearrange(
        "p (t c) -> p t c", t=5)
    zfm = hga[:, 1, 0:10, :].rearrange("p a b -> p (a b)").rearrange(
        "p (j c) -> p j c", j=8)
    sqf = hga[:, 2, 0:10, :].rearrange("p a b -> p (a b)").rearrange(
        "p (j c) -> p j c", j=8)
    w1a = [p.sb([128, 8, 128], BF16, f"w1a{i}") for i in range(2)]
    w1g = [p.sb([128, 8, 128], BF16, f"w1g{i}") for i in range(2)]
    w2c = [p.sb([128, 512], BF16, f"w2c{i}") for i in range(2)]
    tmpa = p.sb([128, 512], BF16, "tmpa")
    tmpg = p.sb([128, 512], BF16, "tmpg")
    tmpf = p.sb([128, 512], F32, "tmpf")
    rowsb = p.sb([1, 2048], F32, "rowsb")
    xg = p.sb([128, 8, DM], F32, "xg")
    xa = zt[:, 0:2, :]
    b1a = p.sb([128, 22], F32, "b1a")
    b1g = p.sb([128, 22], F32, "b1g")
    b2 = p.sb([1, DM], BF16, "b2")
    lng = p.sb([128, 8], F32, "lng")
    lnb = p.sb([128, 8], F32, "lnb")
    ident = p.sb([128, 128], BF16, "ident")
    onesr = p.sb([1, 128], BF16, "onesr")
    onesf = p.sb([1, 128], F32, "onesf")
    onesb = p.sb([128, 1], BF16, "onesb")

    pall = p.psum([128, 4096], F32, "pall")
    tp_ps = pall[:, 0:256].bitcast(BF16)

    tk = {}
    for n, sb_, h_, q in [("b1a", b1a, b1a_h, "Q"), ("b1g", b1g, b1g_h, "Q"),
                          ("b2", b2, b2_h, "P"), ("lng", lng, g_h, "Q"),
                          ("lnb", lnb, bl_h, "Q"), ("ident", ident, id_h, "Q"),
                          ("onesr", onesr, on_h, "Q"),
                          ("onesf", onesf, onf_h, "Q"),
                          ("onesb", onesb, ob_h, "Q")]:
        tk[n] = p.dma(q, sb_[:, :], h_[:, :])

    fr = {}
    store_toks = []

    xn_done = []
    for b in range(B):
        src = bass.AP(zin_h.tensor, b * 640 * DM,
                      [[DM, 128], [128 * DM, 5], [1, DM]])
        tk_z = p.dma("Q", zt[:, :, :], src, after=(fr.get("zt"),))
        cast = p.on("V", lambda E: E.tensor_copy(
            zbf[:, :, :], zt[:, :, :]), after=(tk_z, fr.get("zbf")), inc=1)
        fr["zt"] = cast
        tp_toks = []
        free_tok = fr.get("zfm")
        for t in range(5):
            for j in range(8):
                blk = j % 4
                tt = p.on("T", lambda E, j=j, t=t, blk=blk: E.transpose(
                    tp_ps[:, 128 * blk:128 * blk + 128],
                    zbf[:, t, 128 * j:128 * j + 128], ident[:, :]),
                    after=(cast, tk["ident"]) + p.pw(0),
                    inc=(1 if blk == 3 else 0))
                if blk == 3:
                    te = p.on("S", lambda E, j=j, t=t: E.activation(
                        zfm[:, j - 3:j + 1, 128 * t:128 * t + 128],
                        tp_ps[:, :].rearrange("p (a c) -> p a c", a=4),
                        AF.Identity), after=(tt, free_tok), inc=1)
                    free_tok = None
                    p.pset([0], te)
                    tp_toks.append(te)
        tp_last = tp_toks[-1]
        fr["zbf"] = tp_last
        fr["zfm"] = None

        sq_toks = []
        for j in range(8):
            sq_toks.append(p.on("S", lambda E, j=j: E.activation(
                sqf[:, j, :], zfm[:, j, :], AF.Square),
                after=(tp_last, fr.get("sqf")), inc=1))
        fr["sqf"] = None

        srcs = [(zfm, 0, 128, 4), (zfm, 4, 127, 5), (sqf, 0, 128, 6),
                (sqf, 4, 127, 7)]
        ev_toks = []
        for si_, (ten, j0, coff, bank) in enumerate(srcs):
            mm = None
            for j in range(j0, j0 + 4):
                mm = p.on("T", lambda E, ten=ten, j=j, coff=coff, bank=bank,
                          j0=j0: E.matmul(
                              pall[0:1, 512 * bank:512 * bank + 512],
                              onesb[:, :], ten[:, j, coff:coff + 512],
                              start=(j == j0), stop=(j == j0 + 3)),
                          after=(tp_last, sq_toks[-1], tk["onesb"])
                          + p.pw(bank), inc=(1 if j == j0 + 3 else 0))
            ev = p.on("S", lambda E, si_=si_, bank=bank: E.activation(
                rowsb[:, 512 * si_:512 * (si_ + 1)],
                pall[0:1, 512 * bank:512 * bank + 512], AF.Identity),
                after=(mm, fr.get("rowsb")), inc=1)
            p.pset([bank], ev)
            ev_toks.append(ev)
        s1_ = p.on("V", lambda E: E.tensor_tensor(
            rowsb[:, 0:512], rowsb[:, 0:512], rowsb[:, 512:1024], ALU.add),
            after=tuple(ev_toks), inc=1)
        s2_ = p.on("V", lambda E: E.tensor_scalar_mul(
            rowsb[:, 0:512], rowsb[:, 0:512], 1.0 / 1024), after=(s1_,), inc=1)
        s3_ = p.on("V", lambda E: E.tensor_tensor(
            rowsb[:, 1024:1536], rowsb[:, 1024:1536], rowsb[:, 1536:2048],
            ALU.add), after=tuple(ev_toks), inc=1)
        s4_ = p.on("S", lambda E: E.activation(
            rowsb[:, 1536:2048], rowsb[:, 0:512], AF.Square),
            after=(s2_, s3_), inc=1)
        s5_ = p.on("V", lambda E: E.scalar_tensor_tensor(
            rowsb[:, 1024:1536], rowsb[:, 1024:1536], 1.0 / 1024,
            rowsb[:, 1536:2048], ALU.mult, ALU.subtract), after=(s4_,), inc=1)
        s6_ = p.on("V", lambda E: E.tensor_scalar_add(
            rowsb[:, 1024:1536], rowsb[:, 1024:1536], 1e-5),
            after=(s5_,), inc=1)
        s7_ = p.on("S", lambda E: E.activation(
            rowsb[:, 1536:2048], rowsb[:, 1024:1536], AF.Ln),
            after=(s6_,), inc=1)
        s8_ = p.on("S", lambda E: E.activation(
            rowsb[:, 512:1024], rowsb[:, 1536:2048], AF.Exp, scale=-0.5),
            after=(s7_,), inc=1)
        rm = p.on("T", lambda E: E.matmul(
            pall[:, 2048:2560], onesf[:, :], rowsb[:, 0:512],
            start=True, stop=True), after=(s8_, tk["onesf"]) + p.pw(4), inc=1)
        rr = p.on("T", lambda E: E.matmul(
            pall[:, 2560:3072], onesf[:, :], rowsb[:, 512:1024],
            start=True, stop=True), after=(s8_,) + p.pw(5), inc=1)
        fr["rowsb"] = rr
        xtk = None
        for j in range(8):
            coff = 128 if j < 4 else 127
            t1 = p.on("V", lambda E, j=j, coff=coff: E.tensor_tensor(
                tmpf[:, :], zfm[:, j, coff:coff + 512], pall[:, 2048:2560],
                ALU.subtract), after=(rm, xtk, fr.get("tmpf")), inc=1)
            t2 = p.on("V", lambda E: E.tensor_tensor(
                tmpf[:, :], tmpf[:, :], pall[:, 2560:3072], ALU.mult),
                after=(rr, t1), inc=1)
            xtk = p.on("V", lambda E, j=j, b=b: E.scalar_tensor_tensor(
                xn[:, b, j, :], tmpf[:, :], lng[:, j:j + 1],
                lnb[:, j:j + 1].to_broadcast([128, 512]), ALU.mult, ALU.add),
                after=(t2, tk["lng"], tk["lnb"], fr.get("xn")), inc=1)
        fr["tmpf"] = xtk
        p.pset([4], xtk)
        p.pset([5], xtk)
        xn_done.append(xtk)
    fr["xn"] = None
    xn_all = tuple(xn_done)

    w1a_free = [None, None]
    w1g_free = [None, None]
    prod_toks = []
    for oc in range(22):
        buf = oc % 2
        tka = p.dma("P", w1a[buf][:, :, :], w1_h[0, oc, :, :, :],
                    after=(w1a_free[buf],))
        tkg = p.dma("P", w1g[buf][:, :, :], w1_h[1, oc, :, :, :],
                    after=(w1g_free[buf],))
        for b in range(B):
            bank = b % 2
            ps_ = pall[:, 512 * bank:512 * (bank + 1)]
            mm = None
            for j in range(8):
                mm = p.on("T", lambda E, j=j, b=b, buf=buf, ps_=ps_: E.matmul(
                    ps_, w1a[buf][:, j, :], xn[:, b, j, :],
                    start=(j == 0), stop=(j == 7)),
                    after=(tka,) + xn_all + p.pw(bank),
                    inc=(1 if j == 7 else 0))
            ea = p.on("S", lambda E, oc=oc, ps_=ps_: E.activation(
                tmpa[:, :], ps_, AF.Identity, bias=b1a[:, oc:oc + 1]),
                after=(mm, tk["b1a"], fr.get("tmpa")), inc=1)
            p.pset([bank], ea)
            bank2 = 2 + b % 2
            ps2_ = pall[:, 512 * bank2:512 * (bank2 + 1)]
            mm2 = None
            for j in range(8):
                mm2 = p.on("T", lambda E, j=j, b=b, buf=buf, ps2_=ps2_:
                           E.matmul(ps2_, w1g[buf][:, j, :], xn[:, b, j, :],
                                    start=(j == 0), stop=(j == 7)),
                           after=(tkg,) + xn_all + p.pw(bank2),
                           inc=(1 if j == 7 else 0))
            eg = p.on("S", lambda E, oc=oc, ps2_=ps2_: E.activation(
                tmpg[:, :], ps2_, AF.Gelu, bias=b1g[:, oc:oc + 1]),
                after=(mm2, tk["b1g"], fr.get("tmpg")), inc=1)
            p.pset([bank2], eg)
            pr = p.on("P", lambda E, oc=oc, b=b: E.tensor_mul(
                hga[:, b, oc, :], tmpa[:, :], tmpg[:, :]),
                after=(ea, eg, fr.get("hga")), inc=1)
            prod_toks.append(pr)
            fr["tmpa"] = pr
            fr["tmpg"] = pr
            w1a_free[buf] = mm
            w1g_free[buf] = mm2
    fr["hga"] = None
    w1_done = prod_toks[-1]

    w2_free = [None, None]
    for wave in range(2):
        slots = [(b, qt) for b in (2 * wave, 2 * wave + 1) for qt in range(4)]
        ad_tok = {}
        for dh in range(2):
            mm_last = {}
            for si, (b, qt) in enumerate(slots):
                ps_ = pall[:, 512 * si:512 * (si + 1)]
                p.on("T", lambda E, si=si, dh=dh, ps_=ps_: E.matmul(
                    ps_, onesr[:, :], b2[:, 512 * dh:512 * (dh + 1)],
                    start=True, stop=False),
                    after=(tk["onesr"], tk["b2"], w1_done) + p.pw(si), inc=0)
            for c in range(22):
                buf = c % 2
                tkw = p.dma("P", w2c[buf][:, :], w2_h[c, dh, :, :],
                            after=(w2_free[buf],))
                for si, (b, qt) in enumerate(slots):
                    ps_ = pall[:, 512 * si:512 * (si + 1)]
                    mm_last[(b, qt)] = p.on(
                        "T", lambda E, c=c, b=b, qt=qt, buf=buf, ps_=ps_:
                        E.matmul(ps_, hga[:, b, c, 128 * qt:128 * qt + 128],
                                 w2c[buf][:, :], start=False, stop=(c == 21)),
                        after=(tkw, w1_done),
                        inc=(1 if (c == 21 or si == 7) else 0))
                w2_free[buf] = mm_last[slots[-1]]
            for si, (b, qt) in enumerate(slots):
                ps_ = pall[:, 512 * si:512 * (si + 1)]
                bi = si
                if dh == 0:
                    srcr = bass.AP(zin_h.tensor,
                                   (b * 640 + 128 + 128 * qt) * DM,
                                   [[DM, 128], [1, DM]])
                    tk_r = p.dma("Q", xg[:, bi, :], srcr,
                                 after=(fr.get(f"xg{bi}"), w1_done))
                    fr[(b, qt, "r")] = tk_r
                ad = p.on("V", lambda E, bi=bi, dh=dh, ps_=ps_:
                          E.tensor_tensor(
                              xg[:, bi, 512 * dh:512 * (dh + 1)], ps_,
                              xg[:, bi, 512 * dh:512 * (dh + 1)], ALU.add),
                          after=(mm_last[(b, qt)], fr.get((b, qt, "r"))),
                          inc=1)
                p.pset([si], ad)
                ad_tok[(b, qt, dh)] = ad
        for si, (b, qt) in enumerate(slots):
            bi = si
            t = b * 4 + qt
            adds = (ad_tok[(b, qt, 0)], ad_tok[(b, qt, 1)])
            if final:
                ba = si % 2
                srca = bass.AP(xa_h.tensor, t * 128 * DM, [[DM, 128], [1, DM]])
                tk_a = p.dma("Q", xa[:, ba, :], srca,
                             after=(fr.get(f"xa{ba}"), w1_done))
                fin = p.on("V", lambda E, bi=bi, ba=ba: E.tensor_tensor(
                    xg[:, bi, :], xg[:, bi, :], xa[:, ba, :], ALU.add),
                    after=adds + (tk_a,), inc=1)
                fr[f"xa{ba}"] = fin
                adds = (fin,)
            dst = bass.AP(y_h.tensor, t * 128 * DM, [[DM, 128], [1, DM]])
            st = p.dma("Q", dst, xg[:, bi, :], after=adds)
            store_toks.append(st)
            fr[f"xg{bi}"] = st

    return p.finalize(store_toks)


def _l1fffn_inputs(core, z_full, W1, b1_, W2, b2_, lng_, lnb_, xadd=None):
    z_full = np.asarray(z_full, dtype=np.float32)
    s0 = SPC * core
    zin = np.zeros((B, 640, DM), dtype=np.float32)
    lo = max(0, s0 - 128)
    zin[:, 128 - (s0 - lo):128, :] = z_full[:, lo:s0, :]
    zin[:, 128:, :] = z_full[:, s0:s0 + 512, :]
    W1 = np.asarray(W1)
    w1pad = np.zeros((DM, 2, 2816), dtype=np.float32)
    w1pad[:, 0, :2730] = W1[:, :2730]
    w1pad[:, 1, :2730] = W1[:, 2730:]
    w1dev = np.ascontiguousarray(
        w1pad.reshape(8, 128, 2, 22, 128).transpose(2, 3, 1, 0, 4))
    b1p = np.zeros((2, 2816), dtype=np.float32)
    b1p[0, :2730] = np.asarray(b1_)[:2730]
    b1p[1, :2730] = np.asarray(b1_)[2730:]
    W2 = np.asarray(W2)
    w2pad = np.zeros((2816, DM), dtype=np.float32)
    w2pad[:2730] = W2
    w2dev = np.ascontiguousarray(
        w2pad.reshape(22, 128, 2, 512).transpose(0, 2, 1, 3))
    fm8 = lambda v: np.ascontiguousarray(
        np.asarray(v).reshape(8, 128).T.astype(np.float32))
    d = {
        "zin": zin,
        "w1": w1dev,
        "b1a": np.ascontiguousarray(b1p[0].reshape(22, 128).T),
        "b1g": np.ascontiguousarray(b1p[1].reshape(22, 128).T),
        "w2": w2dev,
        "b2": np.asarray(b2_).reshape(1, DM).astype(np.float32),
        "lng": fm8(lng_), "lnb": fm8(lnb_),
        "ident": np.eye(128, dtype=np.float32).astype(bf16),
        "onesr": np.ones((1, 128), dtype=np.float32).astype(bf16),
        "onesf": np.ones((1, 128), dtype=np.float32),
        "onesb": np.ones((128, 1), dtype=np.float32).astype(bf16),
    }
    if xadd is not None:
        d["xadd"] = np.ascontiguousarray(
            np.asarray(xadd)[:, s0:s0 + 512, :].reshape(TPC, DM).astype(
                np.float32))
    return d


def run_l1f_ffn(z_full, W1, b1_, W2, b2_, lng_, lnb_, xadd=None, cores=None):
    nc = build_l1f_ffn(final=xadd is not None)
    cores = cores if cores is not None else list(range(NCORE))
    maps = [_l1fffn_inputs(c, z_full, W1, b1_, W2, b2_, lng_, lnb_, xadd)
            for c in cores]
    res = run_bass_kernel_spmd(nc, maps, core_ids=cores)
    if len(cores) == NCORE:
        return _unshard_tokens([r["y"] for r in res.results])
    return [np.asarray(r["y"]) for r in res.results]


def build_l1f_attn():
    p = Prog("l1fattn")
    zin_h = p.din("zin", [B, 896, DM])
    wqkv_h = p.din("wqkv", [128, 8, 3072])
    wo_h = p.din("wo", [128, 8, DM])
    tabs_h = p.din("ropetabs", [4, 128, 7, 64])
    scl_h = p.din("sclrows", [1, 4, 64])
    msk_h = p.din("mask", [128, 4, 512], BF16)
    id_h = p.din("ident", [128, 128], BF16)
    onf_h = p.din("onesf", [1, 128])
    y_h = p.dout("y", [TPC, DM])

    zbf = p.sb([128, 7, DM], BF16, "zbf")
    zfm = p.sb([128, 8, 896], BF16, "zfm")
    wqkv = p.sb([128, 8, 3072], BF16, "wqkv")
    wo = p.sb([128, 8, DM], BF16, "wo")
    qk_tm = p.sb([128, 2, 2048], F32, "qktm")
    ta = p.sb([128, 2, 1024], F32, "ta")
    tb = p.sb([128, 1024], F32, "tb")
    rno = p.sb([128, 2, 32], F32, "rno")
    qp = p.sb([128, 6, 2048], BF16, "qp")
    vbf = p.sb([128, 6, 1024], BF16, "vbf")
    qkfm = p.sb([128, 16, 768], BF16, "qkfm")
    probs = p.sb([128, 4, 512], BF16, "probs")
    pTbf = p.sb([128, 512], BF16, "pTbf")
    dmat = p.sb([128, 4, 128], BF16, "dmat")
    ctxa = p.sb([128, 8, 512], BF16, "ctxa")
    tabs = p.sb([128, 4, 7, 64], F32, "tabs")
    reps = p.sb([128, 4, 64], F32, "reps")
    sclr = p.sb([1, 4, 64], F32, "sclr")
    msk = p.sb([128, 4, 512], BF16, "msk")
    ncol = p.sb([128, 32], F32, "ncol")
    xg = zbf[:, 0:4, :].rearrange("p a b -> p (a b)").bitcast(F32).rearrange(
        "p (a b) -> p a b", a=2)
    ident = p.sb([128, 128], BF16, "ident")
    onesf = pTbf[0:1, 0:256].bitcast(F32)

    pall = p.psum([128, 4096], F32, "pall")
    tp_psd = [pall[:, 3072:3328].bitcast(BF16), pall[:, 3584:3840].bitcast(BF16)]
    tp_bk = [6, 7]

    tk = {}
    tk["wqkv"] = p.dma("P", wqkv[:, :, :], wqkv_h[:, :, :])
    tk["wo"] = p.dma("P", wo[:, :, :], wo_h[:, :, :])
    tk["msk"] = p.dma("P", msk[:, :, :], msk_h[:, :, :])
    tk["tabs"] = p.dma("Q", tabs[:, :, :, :],
                       bass.AP(tabs_h.tensor, 0,
                               [[7 * 64, 128], [128 * 7 * 64, 4], [64, 7],
                                [1, 64]]))
    tk["sclr"] = p.dma("Q", sclr[:, :, :], scl_h[:, :, :])
    tk["ident"] = p.dma("Q", ident[:, :], id_h[:, :])
    tk["onesf"] = p.dma("Q", onesf[:, :], onf_h[:, :])

    fr = {}
    store_toks = []

    rep_evs = []
    mm = None
    for i4 in range(4):
        mm = p.on("T", lambda E, i4=i4: E.matmul(
            pall[:, 3584 + 64 * i4:3584 + 64 * (i4 + 1)], onesf[:, :],
            sclr[:, i4, :], start=True, stop=True),
            after=(tk["onesf"], tk["sclr"]) + p.pw(7), inc=1)
    rev = p.on("S", lambda E: E.activation(
        reps[:, :, :], pall[:, 3584:3840].rearrange("p (a c) -> p a c", a=4),
        AF.Identity), after=(mm,), inc=1)
    p.pset([7], rev)
    tab_toks = []
    for i4 in range(4):
        tab_toks.append(p.on("V", lambda E, i4=i4: E.tensor_tensor(
            tabs[:, i4, :, :], tabs[:, i4, :, :],
            reps[:, i4, :].unsqueeze(1).to_broadcast([128, 7, 64]), ALU.mult),
            after=(rev, tk["tabs"]), inc=1))
    tab_done = tuple(tab_toks)
    fr[("pt", 0)] = rev
    fr[("pt", 1)] = rev

    for b in range(B):
        src = bass.AP(zin_h.tensor, b * 896 * DM,
                      [[DM, 128], [128 * DM, 7], [1, DM]])
        tk_z = p.dma("P", zbf[:, :, :], src, after=(fr.get("zbf"),))
        tp_toks = []
        free_tok = fr.get("zfm")
        gidx = 0
        for t in range(7):
            for j in range(8):
                blk = j % 4
                par = gidx % 2
                tt2 = p.on("T", lambda E, j=j, t=t, blk=blk, par=par: E.transpose(
                    tp_psd[par][:, 128 * blk:128 * blk + 128],
                    zbf[:, t, 128 * j:128 * j + 128], ident[:, :]),
                    after=(tk_z, tk["ident"]) + p.pw(tp_bk[par]),
                    inc=(1 if blk == 3 else 0))
                if blk == 3:
                    te = p.on("S", lambda E, j=j, t=t, par=par: E.activation(
                        zfm[:, j - 3:j + 1, 128 * t:128 * t + 128],
                        tp_psd[par][:, :].rearrange("p (a c) -> p a c", a=4),
                        AF.Identity), after=(tt2, free_tok), inc=1)
                    free_tok = None
                    p.pset([tp_bk[par]], te)
                    tp_toks.append(te)
                    gidx += 1
        tp_last = tp_toks[-1]
        fr["zbf"] = tp_last
        fr["zfm"] = None

        tpq_toks = []
        rope_done = [None] * 6

        def emit_qkv_rope(tt_):
            tp2 = tt_ % 2
            ev_toks = []
            for sl in range(6):
                ps_ = pall[:, 512 * sl:512 * (sl + 1)]
                mm = None
                for j in range(8):
                    coff = (128 if j < 4 else 127) + 128 * tt_
                    mm = p.on("T", lambda E, j=j, sl=sl, coff=coff, ps_=ps_:
                              E.matmul(ps_, zfm[:, j, coff:coff + 128],
                                       wqkv[:, j, 512 * sl:512 * (sl + 1)],
                                       start=(j == 0), stop=(j == 7)),
                              after=(tp_last, tk["wqkv"]) + p.pw(sl),
                              inc=(1 if j == 7 else 0))
                if sl < 4:
                    ev = p.on("S", lambda E, sl=sl, ps_=ps_, tp2=tp2:
                              E.activation(
                        qk_tm[:, tp2, 512 * sl:512 * (sl + 1)], ps_,
                        AF.Identity), after=(mm, fr.get(("qktm", tp2, sl))),
                        inc=1)
                else:
                    ev = p.on("S", lambda E, sl=sl, tt_=tt_, ps_=ps_:
                              E.activation(
                                  vbf[:, tt_, 512 * (sl - 4):512 * (sl - 3)],
                                  ps_, AF.Identity),
                              after=(mm, fr.get("vbf")), inc=1)
                p.pset([sl], ev)
                ev_toks.append(ev)
            last_rope = None
            for side in range(2):
                off = 1024 * side
                itab = 2 * side
                rn0 = 16 * side
                qv = qk_tm[:, tp2, off:off + 1024]
                q3 = qv.rearrange("p (h d) -> p h d", h=16)
                eqk = (ev_toks[2 * side], ev_toks[2 * side + 1])
                n1 = p.on("S", lambda E, qv=qv, tp2=tp2: E.activation(
                    ta[:, tp2, :], qv, AF.Square),
                    after=eqk + (fr.get(("ta", tp2)),), inc=1)
                n2 = p.on("V", lambda E, rn0=rn0, tp2=tp2: E.tensor_reduce(
                    rno[:, tp2, rn0:rn0 + 16].unsqueeze(2),
                    ta[:, tp2, :].rearrange("p (h d) -> p h d", h=16),
                    AX.X, ALU.add), after=(n1, fr.get(("rno", tp2))), inc=1)
                n3 = p.on("V", lambda E, rn0=rn0, tp2=tp2: E.tensor_scalar_max(
                    rno[:, tp2, rn0:rn0 + 16], rno[:, tp2, rn0:rn0 + 16],
                    1e-24), after=(n2,), inc=1)
                n4 = p.on("S", lambda E, rn0=rn0, tp2=tp2: E.activation(
                    rno[:, tp2, rn0:rn0 + 16], rno[:, tp2, rn0:rn0 + 16],
                    AF.Ln), after=(n3,), inc=1)
                n5 = p.on("S", lambda E, rn0=rn0, tp2=tp2: E.activation(
                    rno[:, tp2, rn0:rn0 + 16], rno[:, tp2, rn0:rn0 + 16],
                    AF.Exp, scale=-0.5), after=(n4,), inc=1)
                A3 = tabs[:, itab, tt_ + 1, :].unsqueeze(1).to_broadcast(
                    [128, 16, 64])
                Bh = tabs[:, itab + 1, tt_ + 1, :]
                r1 = p.on("V", lambda E, q3=q3, A3=A3, tp2=tp2: E.tensor_tensor(
                    ta[:, tp2, :].rearrange("p (h d) -> p h d", h=16), q3, A3,
                    ALU.mult), after=(n2,) + tab_done, inc=1)
                r2a = p.on("V", lambda E, qv=qv, Bh=Bh, tp2=tp2: E.tensor_tensor(
                    tb[:, :].rearrange("p (h d) -> p h d", h=16)[:, :, 0:32],
                    qv.rearrange("p (h d) -> p h d", h=16)[:, :, 32:64],
                    Bh[:, 0:32].unsqueeze(1).to_broadcast([128, 16, 32]),
                    ALU.mult), after=eqk + (fr.get("tb"),), inc=1)
                r2b = p.on("V", lambda E, qv=qv, Bh=Bh, tp2=tp2: E.tensor_tensor(
                    tb[:, :].rearrange("p (h d) -> p h d", h=16)[:, :, 32:64],
                    qv.rearrange("p (h d) -> p h d", h=16)[:, :, 0:32],
                    Bh[:, 32:64].unsqueeze(1).to_broadcast([128, 16, 32]),
                    ALU.mult), after=eqk, inc=1)
                r3 = p.on("V", lambda E, tp2=tp2: E.tensor_tensor(
                    ta[:, tp2, :], ta[:, tp2, :], tb[:, :], ALU.add),
                    after=(r1, r2a, r2b), inc=1)
                r4 = p.on("V", lambda E, tt_=tt_, off=off, rn0=rn0, tp2=tp2:
                          E.tensor_tensor(
                    qp[:, tt_, off:off + 1024].rearrange(
                        "p (h d) -> p h d", h=16),
                    ta[:, tp2, :].rearrange("p (h d) -> p h d", h=16),
                    rno[:, tp2, rn0:rn0 + 16].unsqueeze(2).to_broadcast(
                        [128, 16, 64]), ALU.mult),
                    after=(r3, n5, fr.get("qp")), inc=1)
                last_rope = r4
                fr[("ta", tp2)] = r4
                fr["tb"] = r3
                fr[("rno", tp2)] = r4
                for sl in range(2 * side, 2 * side + 2):
                    fr[("qktm", tp2, sl)] = r3
            rope_done[tt_] = last_rope

        def emit_qp_tp(tt_):
            for half in range(4):
                par = half % 2
                tt3 = None
                for blk in range(4):
                    jj = half * 4 + blk
                    tt3 = p.on("T", lambda E, jj=jj, tt_=tt_, blk=blk, par=par:
                               E.transpose(
                                   tp_psd[par][:, 128 * blk:128 * blk + 128],
                                   qp[:, tt_, 128 * jj:128 * jj + 128],
                                   ident[:, :]),
                               after=(rope_done[tt_],) + p.pw(tp_bk[par]),
                               inc=(1 if blk == 3 else 0))
                te = p.on("S", lambda E, half=half, tt_=tt_, par=par:
                          E.activation(
                    qkfm[:, 4 * half:4 * (half + 1), 128 * tt_:128 * tt_ + 128],
                    tp_psd[par][:, :].rearrange("p (a c) -> p a c", a=4),
                    AF.Identity), after=(tt3, fr.get("qkfm")), inc=1)
                p.pset([tp_bk[par]], te)
                tpq_toks.append(te)

        for step in range(7):
            if step < 6:
                emit_qkv_rope(step)
            if step >= 1:
                emit_qp_tp(step - 1)
        fr["qkfm"] = None
        fr["vbf"] = None
        fr["qp"] = None
        qk_done = tpq_toks[-1]
        evv = qk_done

        ctx_evs = []
        ia = 0
        for h in range(16):
            qrow = 64 * (h % 2)
            oc_q = h // 2
            for pr in range(2):
                kcol0 = 256 * pr
                par = ia % 2
                sc_bank = par
                pt_bank = [2, 3]
                cx_bank = 4 + par
                sm_done = []
                for qq in range(2):
                    qt = 2 * pr + qq
                    ns0 = 16 * par + 4 * qt
                    mm = p.on("T", lambda E, qt=qt, qrow=qrow, oc_q=oc_q,
                              kcol0=kcol0, sc_bank=sc_bank: E.matmul(
                                  pall[:, 512 * sc_bank:512 * (sc_bank + 1)],
                                  qkfm[qrow:qrow + 64, oc_q,
                                       256 + 128 * qt:384 + 128 * qt],
                                  qkfm[qrow:qrow + 64, 8 + oc_q,
                                       kcol0:kcol0 + 512],
                                  start=True, stop=True),
                              after=(qk_done,) + p.pw(sc_bank), inc=1)
                    mk = p.on("V", lambda E, qt=qt, sc_bank=sc_bank:
                              E.scalar_tensor_tensor(
                        pall[:, 512 * sc_bank:512 * (sc_bank + 1)],
                        pall[:, 512 * sc_bank:512 * (sc_bank + 1)], 8.0,
                        msk[:, qt, :], ALU.mult, ALU.add),
                        after=(mm, tk["msk"]), inc=1)
                    ns = p.on("V", lambda E, ns0=ns0, sc_bank=sc_bank:
                              E.reduce_max(
                        ncol[:, ns0:ns0 + 1],
                        pall[:, 512 * sc_bank:512 * (sc_bank + 1)], AX.X,
                        negate=True), after=(mk, fr.get(("nc", par, qt))),
                        inc=1)
                    ex = p.on("S", lambda E, ns0=ns0, qq=qq, par=par,
                              sc_bank=sc_bank: E.activation(
                        probs[:, 2 * par + qq, :],
                        pall[:, 512 * sc_bank:512 * (sc_bank + 1)], AF.Exp,
                        scale=1.0, bias=ncol[:, ns0:ns0 + 1],
                        accum_out=ncol[:, ns0 + 1:ns0 + 2]),
                        after=(ns, fr.get(("pb", par, qq))), inc=1)
                    p.pset([sc_bank], ex)
                    rd = p.on("V", lambda E, ns0=ns0: E.reciprocal(
                        ncol[:, ns0 + 2:ns0 + 3], ncol[:, ns0 + 1:ns0 + 2]),
                        after=(ex,), inc=1)
                    dm = p.on("V", lambda E, ns0=ns0, par=par, qq=qq:
                              E.tensor_scalar(
                        dmat[:, 2 * par + qq, :], ident[:, :],
                        ncol[:, ns0 + 2:ns0 + 3], None, ALU.mult),
                        after=(rd, fr.get(("dm", par, qq))), inc=1)
                    sm_done.append((ex, dm))
                ctx_mm = None
                ev_l = [None] * 4
                t_mms = None

                def emit_pT4(kap, par=par, sm_done=sm_done):
                    kpar = kap % 2
                    tm = []
                    for qq in range(2):
                        tm.append(p.on(
                            "T", lambda E, kap=kap, qq=qq, par=par, kpar=kpar:
                            E.matmul(
                                pall[:, 512 * pt_bank[kpar] + 128 * qq:
                                      512 * pt_bank[kpar] + 128 * (qq + 1)],
                                probs[:, 2 * par + qq,
                                      128 * kap:128 * kap + 128],
                                dmat[:, 2 * par + qq, :],
                                start=True, stop=True),
                            after=(sm_done[qq][0], sm_done[qq][1])
                            + p.pw(pt_bank[kpar]), inc=(1 if qq == 1 else 0)))
                    ev_l[kap] = p.on("S", lambda E, kpar=kpar: E.activation(
                        pTbf[:, 256 * kpar:256 * (kpar + 1)],
                        pall[:, 512 * pt_bank[kpar]:512 * pt_bank[kpar] + 256],
                        AF.Identity),
                        after=(tm[-1], fr.get(("pt", kpar))), inc=1)
                    p.pset([pt_bank[kpar]], ev_l[kap])
                    return tm

                def emit_ctx4(kap, pr=pr, h=h, cx_bank=cx_bank):
                    kpar = kap % 2
                    ktile = 2 * pr + kap
                    cm = p.on("T", lambda E, ktile=ktile, h=h, kap=kap,
                              kpar=kpar, cx_bank=cx_bank:
                              E.matmul(
                                  pall[0:64, 512 * cx_bank:
                                       512 * cx_bank + 256],
                                  vbf[:, ktile, 64 * h:64 * h + 64],
                                  pTbf[:, 256 * kpar:256 * (kpar + 1)],
                                  start=(kap == 0), stop=(kap == 3)),
                              after=(ev_l[kap], evv) + p.pw(cx_bank),
                              inc=(1 if kap == 3 else 0))
                    fr[("pt", kpar)] = cm
                    return cm

                for kap in range(5):
                    if kap < 4:
                        t_mms = emit_pT4(kap)
                    if kap >= 1:
                        ctx_mm = emit_ctx4(kap - 1)
                ce = p.on("S", lambda E, qrow=qrow, oc_q=oc_q, pr=pr,
                          cx_bank=cx_bank: E.activation(
                              ctxa[qrow:qrow + 64, oc_q,
                                   256 * pr:256 * (pr + 1)],
                              pall[0:64, 512 * cx_bank:512 * cx_bank + 256],
                              AF.Identity),
                          after=(ctx_mm, fr.get(("cx", par))), inc=1)
                p.pset([cx_bank], ce)
                ctx_evs.append(ce)
                for qq in range(2):
                    fr[("pb", par, qq)] = t_mms[-1]
                    fr[("dm", par, qq)] = t_mms[-1]
                    fr[("nc", par, 2 * pr + qq)] = sm_done[qq][1]
                fr[("cx", par)] = ce
                ia += 1
        for key in list(fr):
            if isinstance(key, tuple):
                fr[key] = None
        fr["ctxa"] = None

        omm_last = None
        for qt in range(4):
            t = b * 4 + qt
            bi = qt % 2
            srcx = bass.AP(zin_h.tensor, (b * 896 + 384 + 128 * qt) * DM,
                           [[DM, 128], [1, DM]])
            tk_xr = p.dma("Q", xg[:, bi, :], srcx,
                          after=(fr.get(f"xg{bi}"), tp_last, qk_done))
            ps_o = pall[:, 1024:2048]
            mm = None
            for sl in range(2):
                for j in range(8):
                    mm = p.on("T", lambda E, sl=sl, j=j, qt=qt: E.matmul(
                        ps_o[:, sl * 512:(sl + 1) * 512],
                        ctxa[:, j, 128 * qt:128 * qt + 128],
                        wo[:, j, sl * 512:(sl + 1) * 512],
                        start=(j == 0), stop=(j == 7)),
                        after=tuple(ctx_evs) + (tk["wo"],) + p.pw(2, 3),
                        inc=(1 if j == 7 else 0))
            orr = p.on("V", lambda E, bi=bi: E.tensor_tensor(
                xg[:, bi, :], ps_o, xg[:, bi, :], ALU.add),
                after=(mm, tk_xr), inc=1)
            p.pset([2, 3], orr)
            dst = bass.AP(y_h.tensor, t * 128 * DM, [[DM, 128], [1, DM]])
            st = p.dma("Q", dst, xg[:, bi, :], after=(orr,))
            store_toks.append(st)
            fr[f"xg{bi}"] = st
            omm_last = mm
        fr["ctxa"] = omm_last
        fr["zbf"] = p.barrier_token("Q", tuple(store_toks[-4:]))

    return p.finalize(store_toks)


def _l1fattn_inputs(core, z_full, Wqkv, qs, ks, Wo):
    z_full = np.asarray(z_full, dtype=np.float32)
    s0 = SPC * core
    zin = np.zeros((B, 896, DM), dtype=np.float32)
    lo = max(0, s0 - 384)
    zin[:, 384 - (s0 - lo):384, :] = z_full[:, lo:s0, :]
    zin[:, 384:, :] = z_full[:, s0:s0 + 512, :]
    r = lambda w, no: np.ascontiguousarray(
        np.asarray(w).reshape(8, 128, no).transpose(1, 0, 2))
    pos = np.arange(s0 - 384, s0 + 512, dtype=np.float64)
    half = np.arange(0, 64, 2, dtype=np.float64)
    inv_freq = 1.0 / 10000.0 ** (half / 64.0)
    fr_ = pos[:, None] * inv_freq[None, :]
    fr_ = np.concatenate([fr_, fr_], -1)
    base = (half + 0.4 * 64.0) / (1.4 * 64.0)
    power = (pos - 2048.0) / 128.0
    scale = base[None, :] ** power[:, None]
    scale = np.concatenate([scale, scale], -1)
    sgn = np.where(np.arange(64) < 32, -1.0, 1.0)[None, :]
    cq_ = np.cos(fr_) * scale
    bq_ = np.sin(fr_) * scale * sgn
    ck_ = np.cos(fr_) / scale
    bk_ = np.sin(fr_) / scale * sgn
    pad = pos < 0
    for tbl in (cq_, bq_, ck_, bk_):
        tbl[pad] = 0.0
    tabs = np.stack([cq_, bq_, ck_, bk_]).astype(np.float32)
    tabs = np.ascontiguousarray(
        tabs.reshape(4, 7, 128, 64).transpose(0, 2, 1, 3))
    qs = np.asarray(qs).astype(np.float32)
    ks = np.asarray(ks).astype(np.float32)
    perm = (np.arange(64) + 32) % 64
    sclrows = np.stack([qs, qs[perm], ks, ks[perm]]).reshape(1, 4, 64)
    mask = np.zeros((128, 4, 512), dtype=np.float32)
    row = np.arange(128)[:, None]
    kr = np.arange(512)[None, :]
    for u in range(4):
        qw = 128 * (u % 2) + row
        d = qw + 256 - kr
        bad = (d < 0) | (d > 256)
        w_glob = (512 * core + 128 * u) // 256
        if w_glob == 0:
            bad = bad | (kr < 256)
        mask[:, u, :] = np.where(bad, -1e30, 0.0)
    return {
        "zin": zin, "wqkv": r(Wqkv, 3072), "wo": r(Wo, DM),
        "ropetabs": tabs, "sclrows": np.ascontiguousarray(sclrows),
        "mask": mask.astype(bf16),
        "ident": np.eye(128, dtype=np.float32).astype(bf16),
        "onesf": np.ones((1, 128), dtype=np.float32),
    }


def run_l1f_attn(z_full, Wqkv, qs, ks, Wo, cores=None):
    nc = build_l1f_attn()
    cores = cores if cores is not None else list(range(NCORE))
    maps = [_l1fattn_inputs(c, z_full, Wqkv, qs, ks, Wo) for c in cores]
    res = run_bass_kernel_spmd(nc, maps, core_ids=cores)
    if len(cores) == NCORE:
        return _unshard_tokens([r["y"] for r in res.results])
    return [np.asarray(r["y"]) for r in res.results]


def run_l1f_pre(x_full, cores=None):
    nc = build_l1f_pre()
    shards = _shard_tokens(x_full)
    cores = cores if cores is not None else list(range(NCORE))
    maps = [{"x": shards[c]} for c in cores]
    res = run_bass_kernel_spmd(nc, maps, core_ids=cores)
    return _unshard_tokens([r["y"] for r in res.results])


def _l1ffn_block(z, x_res, zi):
    zc = run_l1f_pre(x_res)
    for i in range(6):
        zc = run_l1f_attn(zc, zi["Wqkv"][i], zi["q_scale"][i],
                          zi["k_scale"][i], zi["Wo"][i])
        zc = run_l1f_ffn(zc, zi["W1"][i], zi["b1"][i], zi["W2"][i],
                         zi["b2"][i], zi["ln_g"][i], zi["ln_b"][i],
                         xadd=x_res if i == 5 else None)
    return zc


def kernel(**ins):
    x = np.asarray(ins["L0_attn_a_hidden_states"], dtype=np.float32)
    x = run_l0_attn_a(
        x, ins["L0_attn_a_wq"], ins["L0_attn_a_bq"], ins["L0_attn_a_wk"],
        ins["L0_attn_a_bk"], ins["L0_attn_a_wv"], ins["L0_attn_a_bv"],
        ins["L0_attn_a_wo"], ins["L0_attn_a_bo"], ins["L0_attn_a_wfq"],
        ins["L0_attn_a_bfq"], ins["L0_attn_a_wfg"], ins["L0_attn_a_bfg"],
        ins["L0_attn_a_up"], ins["L0_attn_a_uz"])
    x = run_l0_ffn(x, ins["L0_ffn_w_router"], ins["L0_ffn_w1"],
                   ins["L0_ffn_b1"], ins["L0_ffn_w2"], ins["L0_ffn_b2"])
    x = run_l0_attn_b(x, ins["L0_attn_b_Wq"], ins["L0_attn_b_bq"],
                      ins["L0_attn_b_Wdown"], ins["L0_attn_b_bdown"],
                      ins["L0_attn_b_Wup"], ins["L0_attn_b_bup"],
                      ins["L0_attn_b_Wout"], ins["L0_attn_b_bout"])
    x = run_l0_ffn(x, ins["L0_ffn_w_router"], ins["L0_ffn_w1"],
                   ins["L0_ffn_b1"], ins["L0_ffn_w2"], ins["L0_ffn_b2"])
    x = run_l1_attn_a(x, ins["L1_attn_a_Wq"], ins["L1_attn_a_bq"],
                      ins["L1_attn_a_Wk"], ins["L1_attn_a_bk"],
                      ins["L1_attn_a_Wv"], ins["L1_attn_a_bv"],
                      ins["L1_attn_a_Wf"], ins["L1_attn_a_bf"],
                      ins["L1_attn_a_Wo"], ins["L1_attn_a_bo"])
    zi = {k: ins[f"L1_ffn_{k}"] for k in
          ["Wqkv", "q_scale", "k_scale", "Wo", "ln_g", "ln_b", "W1", "b1",
           "W2", "b2"]}
    x = _l1ffn_block(None, x, zi)
    x = run_l1_attn_b(x, ins["L1_attn_b_w_in"], ins["L1_attn_b_b_in"],
                      ins["L1_attn_b_w_out"], ins["L1_attn_b_b_out"])
    x = _l1ffn_block(None, x, zi)
    return np.ascontiguousarray(x)
